# version 01
# baseline (speedup 1.0000x reference)
import os
import numpy as np
from contextlib import ExitStack

import concourse.bass as bass
import concourse.mybir as mybir
from concourse.bass_utils import run_bass_kernel_spmd

F32 = mybir.dt.float32
F32R = mybir.dt.float32r
BF16 = mybir.dt.bfloat16
FP8 = mybir.dt.float8e4
U8 = mybir.dt.uint8
AF = mybir.ActivationFunctionType
ALU = mybir.AluOpType
AX = mybir.AxisListType.X

B, L, D, R, DH, HID = 2, 4096, 2048, 2048, 512, 8192
NCORE = 8
TOK = 1024
CH = 16
HCH = 4
JCH = 64
COLS = 1028
CB = COLS * 4
REGB = CH * CB
RL, RH = 2, 1026
DENOM = float(np.float64(np.sqrt(np.float32(D))) / L)
EPS = 1e-5

S_BU, S_BV, S_CB, S_BG2, S_BG1, S_CW = 0, 16, 32, 48, 64, 68

ENGINES = ("tensor", "vector", "scalar", "gpsimd", "sync")

N_PHASES = int(os.environ.get("KPHASES", "99"))
PARANOID = os.environ.get("KPAR", "0") == "1"
DMASER = os.environ.get("KDMASER", "1") == "1"


class Buf:
    __slots__ = ("name", "writer", "readers")

    def __init__(self, name="b", seed=()):
        self.name = name
        self.writer = None
        self.readers = list(seed)


class Prog:
    def __init__(self):
        self.counts = {e: 0 for e in ENGINES}
        self.marks = {e: {p: 0 for p in ENGINES} for e in ENGINES}
        self.ops = {e: [] for e in ENGINES}

    def emit(self, eng, fn, reads=(), writes=(), dma=False):
        need = {}
        for b in reads:
            if b.writer is not None:
                pe, pc = b.writer
                need[pe] = max(need.get(pe, 0), pc)
        for b in writes:
            if b.writer is not None:
                pe, pc = b.writer
                need[pe] = max(need.get(pe, 0), pc)
            for (pe, pc) in b.readers:
                need[pe] = max(need.get(pe, 0), pc)
        if PARANOID:
            for pe in ENGINES:
                if pe != eng:
                    need[pe] = max(need.get(pe, 0), self.counts[pe])
        if DMASER and dma:
            need[eng] = max(need.get(eng, 0), self.counts[eng])
        waits = []
        for pe in ENGINES:
            pc = need.get(pe, 0)
            if pc > self.marks[eng][pe]:
                waits.append((pe, pc))
                self.marks[eng][pe] = pc
        inc = 16 if dma else 1
        self.counts[eng] += inc
        self.ops[eng].append((fn, waits, inc))
        tok = (eng, self.counts[eng])
        for b in reads:
            b.readers.append(tok)
        for b in writes:
            b.writer = tok
            b.readers = []
        return tok

    def build(self, nc, sems):
        with nc.Block() as block:
            def mk(name):
                def run(engine):
                    for (fn, waits, inc) in self.ops[name]:
                        for (pe, pc) in waits:
                            engine.wait_ge(sems[pe], pc)
                        fn(engine).then_inc(sems[name], inc)
                return run
            block.tensor(mk("tensor"))
            block.vector(mk("vector"))
            block.scalar(mk("scalar"))
            block.gpsimd(mk("gpsimd"))
            block.sync(mk("sync"))


class Region:

    def __init__(self, base, base_esz=4):
        self.base = base
        self.base_esz = base_esz
        self.live = []
        self.floor = {}

    def fence(self):
        for b in self.live:
            toks = list(b.readers)
            if b.writer is not None:
                toks.append(b.writer)
            for (pe, pc) in toks:
                self.floor[pe] = max(self.floor.get(pe, 0), pc)
        self.live = []
        return [(pe, pc) for pe, pc in self.floor.items()]

    def buf(self, name, seed):
        b = Buf(name, seed)
        self.live.append(b)
        return b

    def carve(self, off, n, dt, seed, name="c"):
        esz = 4 if dt in (F32, F32R) else (2 if dt is BF16 else 1)
        b = self.base_esz
        assert off % b == 0 and (n * esz) % b == 0
        ap = self.base[:, off // b: off // b + (n * esz) // b]
        if ap.dtype is not dt:
            ap = ap.bitcast(dt)
        return ap, self.buf(name, seed)


def build_nc():
    nc = bass.Bass("TRN2", target_bir_lowering=False, debug=False,
                   dynamic_dma_scratch_size=8192)
    pr = Prog()
    ctx = ExitStack()

    d_in = {}
    for name, shape in [
        ("xT", [D, COLS]),
        ("wu_p", [CH, 128, 2048]), ("wv_p", [CH, 128, 2048]),
        ("wg1_p", [HCH, 128, 2048]), ("wg2_p", [CH, 128, 512]),
        ("wq_p", [CH, 128, 2048]), ("wk_p", [CH, 128, 2048]),
        ("wvb_p", [CH, 128, 2048]), ("wo_p", [CH, 128, 2048]),
        ("g0_p", [JCH, 128, 2048]), ("g1_p", [JCH, 128, 2048]),
        ("lin_p", [CH, 128, 8192]),
        ("smalls", [128, 120]), ("onescol", [128, 2]),
        ("onesrow", [1, 128]), ("ones82", [8, 2]), ("mask8", [8, 1]),
    ]:
        d_in[name] = nc.dram_tensor(name, shape, F32, kind="ExternalInput").ap()
    out_d = nc.dram_tensor("out", [D, TOK], F32, kind="ExternalOutput").ap()

    xsp_d = nc.dram_tensor("xsp", [128, REGB // 4], F32R, kind="Internal").ap()
    ugd_d = nc.dram_tensor("ugd", [R, TOK], F32R, kind="Internal").ap()
    qd_d = nc.dram_tensor("qd", [D, TOK], F32R, kind="Internal").ap()
    cc_in = [nc.dram_tensor(f"ccin{i}", [128, 16], F32, kind="Internal")
             for i in range(2)]
    cc_out = [nc.dram_tensor(f"ccout{i}", [1024, 16], F32, kind="Internal")
              for i in range(2)]

    rx_t = ctx.enter_context(nc.sbuf_tensor("rx", [128, REGB // 4], F32R))
    ry_t = ctx.enter_context(nc.sbuf_tensor("ry", [128, REGB // 4], F32R))
    c1_t = ctx.enter_context(nc.sbuf_tensor("c1", [128, 69632], FP8))
    c2_t = ctx.enter_context(nc.sbuf_tensor("c2", [128, 4096], F32R))
    RX, RY = Region(rx_t[:]), Region(ry_t[:])
    C1, C2 = Region(c1_t[:], base_esz=1), Region(c2_t[:])

    sm_t = ctx.enter_context(nc.sbuf_tensor("smalls_t", [128, 120], F32))
    oc_t = ctx.enter_context(nc.sbuf_tensor("onescol_t", [128, 2], F32R))
    orow_t = ctx.enter_context(nc.sbuf_tensor("onesrow_t", [1, 128], F32R))
    o82_t = ctx.enter_context(nc.sbuf_tensor("ones82_t", [8, 2], F32R))
    msk_t = ctx.enter_context(nc.sbuf_tensor("mask_t", [8, 1], F32))
    tot_t = ctx.enter_context(nc.sbuf_tensor("tot_t", [128, 16], F32))
    car_t = ctx.enter_context(nc.sbuf_tensor("car_t", [128, 32], F32))
    end_t = ctx.enter_context(nc.sbuf_tensor("end_t", [1, 2], F32))

    ps_t = ctx.enter_context(nc.psum_tensor("ps", [128, 4096], F32))
    PS = ps_t[:]

    sems = {e: ctx.enter_context(nc.semaphore(name=f"sem_{e}")) for e in ENGINES}

    xB = [RX.buf(f"x{c}", ()) for c in range(CH)]
    yB = [RY.buf(f"y{c}", ()) for c in range(CH)]
    smB, ocB, orB, o82B, mskB = Buf("sm"), Buf("oc"), Buf("or"), Buf("o82"), Buf("msk")
    totB, carB = Buf("tot"), Buf("car")
    xspB, ugdB, qdB = Buf("xsp"), Buf("ugd"), Buf("qd")
    ccB = [Buf("cc0"), Buf("cc1")]
    outB = Buf("outd")
    psB = [Buf(f"ps{i}") for i in range(4)]
    ps_off = [0, 1024, 2048, 3072]
    ps_rot = [0]

    def ps_next():
        i = ps_rot[0] % 4
        ps_rot[0] += 1
        return PS[:, ps_off[i]:ps_off[i] + 1024], psB[i]

    rxF, ryF = rx_t[:], ry_t[:]

    def xc(c):
        return rxF[:, c * COLS:(c + 1) * COLS]

    def yc(c):
        return ryF[:, c * COLS:(c + 1) * COLS]

    E = pr.emit

    E("scalar", lambda e: e.dma_start(out=sm_t[:], in_=d_in["smalls"]), writes=[smB], dma=True)
    E("scalar", lambda e: e.dma_start(out=oc_t[:], in_=d_in["onescol"].bitcast(F32R)), writes=[ocB], dma=True)
    E("scalar", lambda e: e.dma_start(out=orow_t[:], in_=d_in["onesrow"].bitcast(F32R)), writes=[orB], dma=True)
    E("scalar", lambda e: e.dma_start(out=o82_t[:], in_=d_in["ones82"].bitcast(F32R)), writes=[o82B], dma=True)
    E("scalar", lambda e: e.dma_start(out=msk_t[:], in_=d_in["mask8"]), writes=[mskB], dma=True)
    for c in range(CH):
        E("scalar", lambda e, c=c: e.dma_start(
            out=xc(c), in_=d_in["xT"][c * 128:(c + 1) * 128, :].bitcast(F32R)),
          writes=[xB[c]], dma=True)

    def ln():
        seed = C2.fence()
        rowm = C2.carve(0, COLS, F32R, seed, "rowm")
        rowv = C2.carve(4112, COLS, F32R, seed, "rowv")
        rowr = C2.carve(8224, COLS, F32R, seed, "rowr")
        rm, rv, rr = rowm[0][0:1, :], rowv[0][0:1, :], rowr[0][0:1, :]
        yseed = RY.fence()
        for c in range(CH):
            yB[c] = RY.buf(f"y{c}", yseed)
        splits = [(0, 512), (512, 512), (1024, 4)]
        for c in range(CH):
            E("scalar", lambda e, c=c: e.activation(yc(c), xc(c), AF.Square),
              reads=[xB[c]], writes=[yB[c]])
            for (o, n) in splits:
                E("tensor", lambda e, c=c, o=o, n=n: e.matmul(
                    PS[0:1, o:o + n], oc_t[:][:, 0:1], xc(c)[:, o:o + n],
                    start=(c == 0), stop=(c == CH - 1)),
                  reads=[xB[c], ocB], writes=[psB[0] if o < 1024 else psB[1]])
                E("tensor", lambda e, c=c, o=o, n=n: e.matmul(
                    PS[0:1, 2048 + o:2048 + o + n], oc_t[:][:, 0:1], yc(c)[:, o:o + n],
                    start=(c == 0), stop=(c == CH - 1)),
                  reads=[yB[c], ocB], writes=[psB[2] if o < 1024 else psB[3]])
        E("vector", lambda e, a=rm: e.tensor_scalar_mul(a, PS[0:1, 0:COLS], 1.0 / D),
          reads=[psB[0], psB[1]], writes=[rowm[1]])
        E("vector", lambda e, a=rv: e.tensor_scalar_mul(a, PS[0:1, 2048:2048 + COLS], 1.0 / D),
          reads=[psB[2], psB[3]], writes=[rowv[1]])
        E("vector", lambda e, a=rr, m=rm: e.tensor_mul(a, m, m),
          reads=[rowm[1]], writes=[rowr[1]])
        E("vector", lambda e, a=rv, m2=rr: e.tensor_sub(a, a, m2),
          reads=[rowr[1], rowv[1]], writes=[rowv[1]])
        E("vector", lambda e, a=rv: e.tensor_scalar_add(a, a, EPS),
          reads=[rowv[1]], writes=[rowv[1]])
        E("scalar", lambda e, a=rv: e.activation(a, a, AF.Sqrt),
          reads=[rowv[1]], writes=[rowv[1]])
        E("vector", lambda e, a=rr, v=rv: e.reciprocal(a, v),
          reads=[rowv[1]], writes=[rowr[1]])
        E("vector", lambda e, a=rm, r=rr: e.scalar_tensor_tensor(
            a, a, -1.0, r, ALU.mult, ALU.mult),
          reads=[rowr[1], rowm[1]], writes=[rowm[1]])
        for (o, n) in splits:
            E("tensor", lambda e, o=o, n=n, r=rr: e.matmul(
                PS[:, o:o + n], orow_t[:], r[0:1, o:o + n], start=True, stop=True),
              reads=[rowr[1], orB], writes=[psB[0] if o < 1024 else psB[1]])
            E("tensor", lambda e, o=o, n=n, r=rm: e.matmul(
                PS[:, 2048 + o:2048 + o + n], orow_t[:], r[0:1, o:o + n], start=True, stop=True),
              reads=[rowm[1], orB], writes=[psB[2] if o < 1024 else psB[3]])
        for c in range(CH):
            E("vector", lambda e, c=c: e.tensor_mul(yc(c), xc(c), PS[:, 0:COLS]),
              reads=[xB[c], psB[0], psB[1]], writes=[yB[c]])
            E("vector", lambda e, c=c: e.tensor_add(yc(c), yc(c), PS[:, 2048:2048 + COLS]),
              reads=[psB[2], psB[3], yB[c]], writes=[yB[c]])

    def spill_x():
        E("scalar", lambda e: e.dma_start(out=xsp_d, in_=rx_t[:]),
          reads=list(xB), writes=[xspB], dma=True)

    def restore_x(seed):
        for c in range(CH):
            xB[c] = RX.buf(f"x{c}", seed)
            E("scalar", lambda e, c=c: e.dma_start(
                out=xc(c), in_=xsp_d[:, c * COLS:(c + 1) * COLS]),
              reads=[xspB], writes=[xB[c]], dma=True)

    def carry_exchange(idx):
        cin, cout = cc_in[idx].ap(), cc_out[idx].ap()
        gma, gmb = RY.carve(4096 * 4, 2048, F32R, RY.fence(), "gm")
        gm = gma[0:8, :]
        E("scalar", lambda e, cin=cin: e.dma_start(out=cin, in_=tot_t[:]),
          reads=[totB], writes=[ccB[idx]], dma=True)
        E("gpsimd", lambda e, i=idx: e.collective_compute(
            "AllGather", ALU.bypass, [list(range(NCORE))],
            [cc_in[i].ap().opt()], [cc_out[i].ap().opt()]),
          reads=[ccB[idx]], writes=[ccB[idx]])
        E("scalar", lambda e, cout=cout, gm=gm: e.dma_start(
            out=gm, in_=cout.rearrange("(r p) c -> r (p c)", p=128).bitcast(F32R)),
          reads=[ccB[idx]], writes=[gmb], dma=True)
        E("vector", lambda e, gm=gm: e.tensor_scalar_mul(gm, gm, msk_t[:][:, 0:1]),
          reads=[mskB, gmb], writes=[gmb])
        gmr = gm.rearrange("r (p c) -> r c p", c=16)
        for m in range(CH):
            E("tensor", lambda e, m=m, g=gmr: e.matmul(
                PS[:, 3072 + 2 * m:3072 + 2 * m + 2], g[:, m, :], o82_t[:],
                start=True, stop=True),
              reads=[gmb, o82B], writes=[psB[3]])
        E("vector", lambda e: e.tensor_copy(car_t[:], PS[:, 3072:3104]),
          reads=[psB[3]], writes=[carB])

    def mm_acc(lhs_ap_fn, lhs_buf, rhs_fn, rhs_buf_fn, nk):
        psap, psb = ps_next()
        for k in range(nk):
            for h in range(2):
                E("tensor", lambda e, k=k, h=h, p=psap: e.matmul(
                    p[:, h * 512:(h + 1) * 512], lhs_ap_fn(k), rhs_fn(k, h),
                    start=(k == 0), stop=(k == nk - 1)),
                  reads=[lhs_buf, rhs_buf_fn(k)], writes=[psb])
        return psap, psb

    def y_rhs(k, h):
        return yc(k)[:, RL + h * 512: RL + (h + 1) * 512]

    def y_buf(k):
        return yB[k]

    ph = [0]

    def phase(f):
        if ph[0] < N_PHASES:
            f()
        ph[0] += 1

    phase(ln)

    def p_conv():
        seed = C1.fence()
        ct = [C1.carve(45056, TOK, F32, seed, "cv0"),
              C1.carve(49152, TOK, F32, seed, "cv1")]
        for c in range(CH):
            a, ab = ct[c % 2]
            cw0 = sm_t[:][:, S_CW + 3 * c: S_CW + 3 * c + 1]
            cw1 = sm_t[:][:, S_CW + 3 * c + 1: S_CW + 3 * c + 2]
            cw2 = sm_t[:][:, S_CW + 3 * c + 2: S_CW + 3 * c + 3]
            cbv = sm_t[:][:, S_CB + c: S_CB + c + 1]
            E("vector", lambda e, c=c, a=a, w=cw0, bv=cbv: e.tensor_scalar(
                a, yc(c)[:, 1:1025], w, bv, ALU.mult, ALU.add),
              reads=[yB[c], smB], writes=[ab])
            E("vector", lambda e, c=c, a=a, w=cw1: e.scalar_tensor_tensor(
                a, yc(c)[:, 2:1026], w, a, ALU.mult, ALU.add),
              reads=[yB[c], smB, ab], writes=[ab])
            E("vector", lambda e, c=c, a=a, w=cw2: e.scalar_tensor_tensor(
                a, yc(c)[:, 3:1027], w, a, ALU.mult, ALU.add),
              reads=[yB[c], smB, ab], writes=[ab])
            E("vector", lambda e, c=c, a=a: e.tensor_add(
                xc(c)[:, RL:RH], xc(c)[:, RL:RH], a),
              reads=[ab, xB[c]], writes=[xB[c]])
    phase(p_conv)

    def p_attn_a_proj():
        spill_x()
        seed = RX.fence()
        wsA = [RX.carve(0, 2048, F32R, seed, "wsA0"),
               RX.carve(8192, 2048, F32R, seed, "wsA1")]
        wsB = [RX.carve(16384, 2048, F32R, seed, "wsB0"),
               RX.carve(24576, 2048, F32R, seed, "wsB1")]
        wg2f = [RX.carve(32768, 512, F32R, seed, "wg2f0"),
                RX.carve(34816, 512, F32R, seed, "wg2f1")]
        gtmp = RX.carve(36864, TOK, F32R, seed, "gtmp")
        ugt = [RX.carve(40960, TOK, F32R, seed, "ugt0"),
               RX.carve(45056, TOK, F32R, seed, "ugt1")]
        cseed = C1.fence()
        vgc = [C1.carve(c * 2048, TOK, BF16, cseed, f"vg{c}") for c in range(CH)]
        hc = [C1.carve(32768 + k * 2048, TOK, BF16, cseed, f"h{k}") for k in range(HCH)]
        wg2b = [C1.carve(40960, 512, BF16, cseed, "wg2b0"),
                C1.carve(41984, 512, BF16, cseed, "wg2b1")]
        for m in range(HCH):
            buf = wsA[m % 2]
            E("sync", lambda e, m=m, a=buf[0]: e.dma_start(
                out=a, in_=d_in["wg1_p"][m].bitcast(F32R)), writes=[buf[1]], dma=True)
            psap, psb = mm_acc(
                lambda k, a=buf[0]: a[:, k * 128:(k + 1) * 128], buf[1],
                y_rhs, y_buf, CH)
            E("scalar", lambda e, m=m, p=psap, a=hc[m][0]: e.activation(
                a, p, AF.Relu, bias=sm_t[:][:, S_BG1 + m:S_BG1 + m + 1]),
              reads=[psb, smB], writes=[hc[m][1]])
        for m in range(CH):
            bufu, bufv = wsA[m % 2], wsB[m % 2]
            bufg, bufgb = wg2f[m % 2], wg2b[m % 2]
            E("sync", lambda e, m=m, a=bufg[0]: e.dma_start(
                out=a, in_=d_in["wg2_p"][m].bitcast(F32R)), writes=[bufg[1]], dma=True)
            E("sync", lambda e, m=m, a=bufu[0]: e.dma_start(
                out=a, in_=d_in["wu_p"][m].bitcast(F32R)), writes=[bufu[1]], dma=True)
            E("sync", lambda e, m=m, a=bufv[0]: e.dma_start(
                out=a, in_=d_in["wv_p"][m].bitcast(F32R)), writes=[bufv[1]], dma=True)
            E("scalar", lambda e, a=bufgb[0], f=bufg[0]: e.copy(a, f),
              reads=[bufg[1]], writes=[bufgb[1]])
            psg, psgb = mm_acc(
                lambda k, a=bufgb[0]: a[:, k * 128:(k + 1) * 128], bufgb[1],
                lambda k, h: hc[k][0][:, h * 512:(h + 1) * 512],
                lambda k: hc[k][1], HCH)
            E("scalar", lambda e, a=gtmp[0], p=psg, m=m: e.activation(
                a, p, AF.Sigmoid, bias=sm_t[:][:, S_BG2 + m:S_BG2 + m + 1]),
              reads=[psgb, smB], writes=[gtmp[1]])
            psu, psub = mm_acc(
                lambda k, a=bufu[0]: a[:, k * 128:(k + 1) * 128], bufu[1],
                y_rhs, y_buf, CH)
            ug = ugt[m % 2]
            E("vector", lambda e, a=ug[0], p=psu, m=m, g=gtmp[0]: e.scalar_tensor_tensor(
                a, p, sm_t[:][:, S_BU + m:S_BU + m + 1], g, ALU.add, ALU.mult),
              reads=[psub, smB, gtmp[1]], writes=[ug[1]])
            E("vector", lambda e, a=ug[0], m=m: e.reduce_sum(
                tot_t[:][:, m:m + 1], a, axis=AX),
              reads=[ug[1]], writes=[totB])
            E("scalar", lambda e, a=ug[0], m=m: e.dma_start(
                out=ugd_d[m * 128:(m + 1) * 128, :], in_=a),
              reads=[ug[1]], writes=[ugdB], dma=True)
            psv, psvb = mm_acc(
                lambda k, a=bufv[0]: a[:, k * 128:(k + 1) * 128], bufv[1],
                y_rhs, y_buf, CH)
            E("vector", lambda e, m=m, p=psv, g=gtmp[0], a=vgc[m][0]: e.scalar_tensor_tensor(
                a, p, sm_t[:][:, S_BV + m:S_BV + m + 1], g, ALU.add, ALU.mult),
              reads=[psvb, smB, gtmp[1]], writes=[vgc[m][1]])
        return vgc
    vgc_holder = []
    phase(lambda: vgc_holder.append(p_attn_a_proj()))

    gc_holder = []

    def p_attn_a_scan():
        vgc = vgc_holder[0]
        carry_exchange(0)
        cseed = C1.fence()
        ugr = [C1.carve(45056, TOK, F32R, cseed, "ugr0"),
               C1.carve(49152, TOK, F32R, cseed, "ugr1")]
        ktmp = C1.carve(53248, TOK, F32R, cseed, "ktmp")
        yseed = RY.fence()
        gc = [RY.carve(c * 4096, TOK, F32R, yseed, f"g{c}") for c in range(CH)]
        gc_holder.append(gc)
        for m in range(CH):
            u = ugr[m % 2]
            E("scalar", lambda e, a=u[0], m=m: e.dma_start(
                out=a, in_=ugd_d[m * 128:(m + 1) * 128, :]),
              reads=[ugdB], writes=[u[1]], dma=True)
            E("vector", lambda e, a=ktmp[0], u=u[0], m=m: e.tensor_tensor_scan(
                a, u, u, car_t[:][:, 2 * m:2 * m + 1], ALU.add, ALU.bypass),
              reads=[u[1], carB], writes=[ktmp[1]])
            E("vector", lambda e, m=m, a=ktmp[0], g=gc[m][0], v=vgc[m][0]: e.tensor_mul(
                g, a, v),
              reads=[ktmp[1], vgc[m][1]], writes=[gc[m][1]])
    phase(p_attn_a_scan)

    def p_attn_a_uv():
        kuv = os.environ.get("KUV", "all")
        gc = gc_holder[0]
        seed = C2.fence()
        ws = [C2.carve(0, 2048, F32R, seed, "uvw0"),
              C2.carve(8192, 2048, F32R, seed, "uvw1")]
        if kuv != "nores":
            restore_x(RX.fence())
        for mo in range(CH):
            if kuv == "restonly":
                continue
            buf = ws[mo % 2]
            E("sync", lambda e, mo=mo, a=buf[0]: e.dma_start(
                out=a, in_=d_in["wu_p"][mo].bitcast(F32R)), writes=[buf[1]], dma=True)
            psap, psb = mm_acc(
                lambda k, a=buf[0]: a[:, k * 128:(k + 1) * 128], buf[1],
                lambda k, h: gc[k][0][:, h * 512:(h + 1) * 512],
                lambda k: gc[k][1], CH)
            if kuv == "all":
                E("vector", lambda e, mo=mo, p=psap: e.tensor_add(
                    xc(mo)[:, RL:RH], xc(mo)[:, RL:RH], p),
                  reads=[psb, xB[mo]], writes=[xB[mo]])
    phase(p_attn_a_uv)

    def p_ffn(final):
        kffn = os.environ.get("KFFN", "all") if final else "all"
        if kffn in ("xpre", "xln", "xspill", "xdump"):
            if kffn == "xln":
                ln()
            if kffn == "xspill":
                spill_x()
                restore_x(RX.fence())
            if kffn == "xdump":
                spill_x()
                for mo in range(CH):
                    E("sync", lambda e, mo=mo: e.dma_start(
                        out=out_d[mo * 128:(mo + 1) * 128, :].bitcast(F32R),
                        in_=xsp_d[:, mo * COLS + RL: mo * COLS + RH]),
                      reads=[xspB], writes=[outB], dma=True)
                E("vector", lambda e: e.memset(end_t[0:1, 0:1], 0.0), reads=[outB])
                return
            yseed = RY.fence()
            otl = [RY.carve(0, TOK, F32R, yseed, "ot0"),
                   RY.carve(4096, TOK, F32R, yseed, "ot1")]
            for mo in range(CH):
                o = otl[mo % 2]
                E("vector", lambda e, mo=mo, a=o[0]: e.tensor_scalar_mul(
                    a, xc(mo)[:, RL:RH], 1.0),
                  reads=[xB[mo]], writes=[o[1]])
                E("sync", lambda e, mo=mo, a=o[0]: e.dma_start(
                    out=out_d[mo * 128:(mo + 1) * 128, :].bitcast(F32R), in_=a),
                  reads=[o[1]], writes=[outB], dma=True)
            return
        ln()
        spill_x()
        xseed = RX.fence()
        g0 = [RX.carve(0, 2048, F32R, xseed, "g0a"),
              RX.carve(8192, 2048, F32R, xseed, "g0b")]
        g1 = [RX.carve(16384, 2048, F32R, xseed, "g1a"),
              RX.carve(24576, 2048, F32R, xseed, "g1b")]
        gel = RX.carve(32768, TOK, F32R, xseed, "gel")
        cseed = C1.fence()
        a8 = [C1.carve(j * 1024, TOK, FP8, cseed, f"a8{j}") for j in range(JCH)]
        for j in range(JCH if kffn != "xonly" else 0):
            b0, b1 = g0[j % 2], g1[j % 2]
            E("sync", lambda e, j=j, a=b0[0]: e.dma_start(
                out=a, in_=d_in["g0_p"][j].bitcast(F32R)), writes=[b0[1]], dma=True)
            E("sync", lambda e, j=j, a=b1[0]: e.dma_start(
                out=a, in_=d_in["g1_p"][j].bitcast(F32R)), writes=[b1[1]], dma=True)
            psg, psgb = mm_acc(
                lambda k, a=b0[0]: a[:, k * 128:(k + 1) * 128], b0[1],
                y_rhs, y_buf, CH)
            E("scalar", lambda e, a=gel[0], p=psg: e.activation(a, p, AF.Gelu_apprx_tanh),
              reads=[psgb], writes=[gel[1]])
            psf, psfb = mm_acc(
                lambda k, a=b1[0]: a[:, k * 128:(k + 1) * 128], b1[1],
                y_rhs, y_buf, CH)
            E("vector", lambda e, j=j, a=gel[0], p=psf, o=a8[j][0]: e.tensor_mul(o, a, p),
              reads=[gel[1], psfb], writes=[a8[j][1]])
        c2seed = C2.fence()
        vfq = C2.carve(0, 2048, F32R, c2seed, "vfq")
        v8 = [C1.carve(65536, 2048, FP8, C1.fence(), "v80"),
              C1.carve(67584, 2048, FP8, (), "v81")]
        otl = None
        if final:
            yseed = RY.fence()
            otl = [RY.carve(0, TOK, F32R, yseed, "ot0"),
                   RY.carve(4096, TOK, F32R, yseed, "ot1")]
        restore_x(RX.fence())
        if kffn == "xonly":
            for mo in range(CH):
                o = otl[mo % 2]
                E("vector", lambda e, mo=mo, a=o[0]: e.tensor_scalar_mul(
                    a, xc(mo)[:, RL:RH], 1.0),
                  reads=[xB[mo]], writes=[o[1]])
                E("sync", lambda e, mo=mo, a=o[0]: e.dma_start(
                    out=out_d[mo * 128:(mo + 1) * 128, :].bitcast(F32R), in_=a),
                  reads=[o[1]], writes=[outB], dma=True)
            return
        for mo in range(CH):
            psap, psb = ps_next()
            for qq in range(4):
                q8 = v8[qq % 2]
                E("sync", lambda e, mo=mo, qq=qq, a=vfq[0]: e.dma_start(
                    out=a, in_=d_in["lin_p"][mo][:, qq * 2048:(qq + 1) * 2048].bitcast(F32R)),
                  reads=[], writes=[vfq[1]], dma=True)
                E("scalar", lambda e, a=q8[0], f=vfq[0]: e.copy(a, f),
                  reads=[vfq[1]], writes=[q8[1]])
                for jj in range(16):
                    j = qq * 16 + jj
                    for h in range(2):
                        E("tensor", lambda e, j=j, jj=jj, h=h, q=q8[0], p=psap: e.matmul(
                            p[:, h * 512:(h + 1) * 512], q[:, jj * 128:(jj + 1) * 128],
                            a8[j][0][:, h * 512:(h + 1) * 512],
                            start=(j == 0), stop=(j == JCH - 1)),
                          reads=[q8[1], a8[j][1]], writes=[psb])
            if final:
                o = otl[mo % 2]
                E("vector", lambda e, mo=mo, p=psap, a=o[0]: e.tensor_add(
                    a, xc(mo)[:, RL:RH], p),
                  reads=[psb, xB[mo]], writes=[o[1]])
                E("sync", lambda e, mo=mo, a=o[0]: e.dma_start(
                    out=out_d[mo * 128:(mo + 1) * 128, :].bitcast(F32R), in_=a),
                  reads=[o[1]], writes=[outB], dma=True)
            else:
                E("vector", lambda e, mo=mo, p=psap: e.tensor_add(
                    xc(mo)[:, RL:RH], xc(mo)[:, RL:RH], p),
                  reads=[psb, xB[mo]], writes=[xB[mo]])
    phase(lambda: p_ffn(False))

    def p_attn_b():
        katb = os.environ.get("KATB", "all")
        ln()
        spill_x()
        xseed = RX.fence()
        wq = [RX.carve(0, 2048, F32R, xseed, "wq0"),
              RX.carve(8192, 2048, F32R, xseed, "wq1")]
        wk = [RX.carve(16384, 2048, F32R, xseed, "wk0"),
              RX.carve(24576, 2048, F32R, xseed, "wk1")]
        wv = [RX.carve(32768, 2048, F32R, xseed, "wv0"),
              RX.carve(40960, 2048, F32R, xseed, "wv1")]
        ktm = RX.carve(49152, TOK, F32R, xseed, "ktm")
        qtm = [RX.carve(53248, TOK, F32R, xseed, "qt0"),
               RX.carve(57344, TOK, F32R, xseed, "qt1")]
        cseed = C1.fence()
        kvc = [C1.carve(c * 4096, TOK, F32, cseed, f"kv{c}") for c in range(CH)]
        if katb == "ln":
            return
        nM = int(katb[1:]) if katb.startswith("m") else CH
        for m in range(nM):
            bq, bk, bv = wq[m % 2], wk[m % 2], wv[m % 2]
            E("sync", lambda e, m=m, a=bq[0]: e.dma_start(
                out=a, in_=d_in["wq_p"][m].bitcast(F32R)), writes=[bq[1]], dma=True)
            E("sync", lambda e, m=m, a=bk[0]: e.dma_start(
                out=a, in_=d_in["wk_p"][m].bitcast(F32R)), writes=[bk[1]], dma=True)
            E("sync", lambda e, m=m, a=bv[0]: e.dma_start(
                out=a, in_=d_in["wvb_p"][m].bitcast(F32R)), writes=[bv[1]], dma=True)
            if katb != "noq":
                psq, psqb = mm_acc(
                    lambda k, a=bq[0]: a[:, k * 128:(k + 1) * 128], bq[1],
                    y_rhs, y_buf, CH)
                if katb != "qne":
                    qb = qtm[m % 2]
                    E("scalar", lambda e, a=qb[0], p=psq: e.activation(a, p, AF.Copy, scale=DENOM),
                      reads=[psqb], writes=[qb[1]])
                    if katb != "qns":
                        E("sync", lambda e, a=qb[0], m=m: e.dma_start(
                            out=qd_d[m * 128:(m + 1) * 128, :], in_=a),
                          reads=[qb[1]], writes=[qdB], dma=True)
            psk, pskb = mm_acc(
                lambda k, a=bk[0]: a[:, k * 128:(k + 1) * 128], bk[1],
                y_rhs, y_buf, CH)
            E("vector", lambda e, a=ktm[0], p=psk: e.tensor_copy(a, p),
              reads=[pskb], writes=[ktm[1]])
            psv, psvb = mm_acc(
                lambda k, a=bv[0]: a[:, k * 128:(k + 1) * 128], bv[1],
                y_rhs, y_buf, CH)
            E("vector", lambda e, m=m, a=ktm[0], p=psv, o=kvc[m][0]: e.tensor_mul(o, a, p),
              reads=[ktm[1], psvb], writes=[kvc[m][1]])
            E("vector", lambda e, m=m, o=kvc[m][0]: e.reduce_sum(
                tot_t[:][:, m:m + 1], o, axis=AX),
              reads=[kvc[m][1]], writes=[totB])
        if katb == "proj":
            return
        carry_exchange(1)
        if katb == "cc":
            return
        yseed = RY.fence()
        vh = [RY.carve(c * 4096, TOK, F32R, yseed, f"vh{c}") for c in range(CH)]
        for m in range(CH):
            qb = qtm[m % 2]
            E("scalar", lambda e, a=qb[0], m=m: e.dma_start(
                out=a, in_=qd_d[m * 128:(m + 1) * 128, :]),
              reads=[qdB], writes=[qb[1]], dma=True)
            E("vector", lambda e, a=ktm[0], m=m, u=kvc[m][0]: e.tensor_tensor_scan(
                a, u, u, car_t[:][:, 2 * m:2 * m + 1], ALU.add, ALU.bypass),
              reads=[kvc[m][1], carB], writes=[ktm[1]])
            E("vector", lambda e, m=m, q=qb[0], a=ktm[0], o=vh[m][0]: e.tensor_mul(o, q, a),
              reads=[qb[1], ktm[1]], writes=[vh[m][1]])
        if katb == "scan":
            return
        c2seed = C2.fence()
        wo = [C2.carve(0, 2048, F32R, c2seed, "wo0"),
              C2.carve(8192, 2048, F32R, c2seed, "wo1")]
        restore_x(RX.fence())
        for mo in range(CH):
            buf = wo[mo % 2]
            E("sync", lambda e, mo=mo, a=buf[0]: e.dma_start(
                out=a, in_=d_in["wo_p"][mo].bitcast(F32R)), writes=[buf[1]], dma=True)
            psap, psb = mm_acc(
                lambda k, a=buf[0]: a[:, k * 128:(k + 1) * 128], buf[1],
                lambda k, h: vh[k][0][:, h * 512:(h + 1) * 512],
                lambda k: vh[k][1], CH)
            E("vector", lambda e, mo=mo, p=psap: e.tensor_add(
                xc(mo)[:, RL:RH], xc(mo)[:, RL:RH], p),
              reads=[psb, xB[mo]], writes=[xB[mo]])
    phase(p_attn_b)

    phase(lambda: p_ffn(True))

    if N_PHASES < 8:
        yseed = RY.fence()
        ot = [RY.carve(0, TOK, F32R, yseed, "dbg0"),
              RY.carve(4096, TOK, F32R, yseed, "dbg1")]
        for mo in range(CH):
            o = ot[mo % 2]
            E("vector", lambda e, mo=mo, a=o[0]: e.tensor_scalar_mul(
                a, xc(mo)[:, RL:RH], 1.0),
              reads=[xB[mo]], writes=[o[1]])
            E("sync", lambda e, mo=mo, a=o[0]: e.dma_start(
                out=out_d[mo * 128:(mo + 1) * 128, :].bitcast(F32R), in_=a),
              reads=[o[1]], writes=[outB], dma=True)

    fin = [(e, pr.counts[e]) for e in ENGINES if e != "vector" and pr.counts[e] > 0]
    endB = Buf("end")
    E("vector", lambda e: e.memset(end_t[:], 0.0), writes=[endB])
    pr.ops["vector"].append((lambda e: e.memset(end_t[:], 1.0), fin, 1))
    pr.counts["vector"] += 1

    with nc.allow_low_precision(reason="deliberate f32r/bf16/fp8 stores; f32 PSUM accumulation throughout"):
        pr.build(nc, sems)
    ctx.close()
    return nc


def _pack2d(A):
    K, M = A.shape
    return np.ascontiguousarray(
        A.reshape(K // 128, 128, M // 128, 128).transpose(2, 1, 0, 3)
        .reshape(M // 128, 128, K))


_NC_CACHE = [None]


def kernel(L0_attn_a_x, L0_attn_a_Wu, L0_attn_a_bu, L0_attn_a_Wv, L0_attn_a_bv,
           L0_attn_a_Wg1, L0_attn_a_bg1, L0_attn_a_Wg2, L0_attn_a_bg2,
           L0_attn_a_conv_w, L0_attn_a_conv_b, L0_attn_b_Wq, L0_attn_b_Wk,
           L0_attn_b_Wv, L0_attn_b_Wo, L0_ffn_w_gating, L0_ffn_w_linear):
    f = np.float32
    x = np.asarray(L0_attn_a_x, f)
    Wu = np.asarray(L0_attn_a_Wu, f); Wv = np.asarray(L0_attn_a_Wv, f)
    Wg1 = np.asarray(L0_attn_a_Wg1, f); Wg2 = np.asarray(L0_attn_a_Wg2, f)
    Wq = np.asarray(L0_attn_b_Wq, f); Wk = np.asarray(L0_attn_b_Wk, f)
    Wvb = np.asarray(L0_attn_b_Wv, f); Wo = np.asarray(L0_attn_b_Wo, f)
    wg = np.asarray(L0_ffn_w_gating, f); wl = np.asarray(L0_ffn_w_linear, f)

    packs = {
        "wu_p": _pack2d(np.ascontiguousarray(Wu.T)),
        "wv_p": _pack2d(np.ascontiguousarray(Wv.T)),
        "wg1_p": _pack2d(np.ascontiguousarray(Wg1.T)),
        "wg2_p": _pack2d(np.ascontiguousarray(Wg2.T)),
        "wq_p": _pack2d(np.ascontiguousarray(Wq.T)),
        "wk_p": _pack2d(np.ascontiguousarray(Wk.T)),
        "wvb_p": _pack2d(np.ascontiguousarray(Wvb.T)),
        "wo_p": _pack2d(np.ascontiguousarray(Wo.T)),
        "g0_p": _pack2d(wg[0]),
        "g1_p": _pack2d(wg[1]),
        "lin_p": _pack2d(wl),
    }
    smalls = np.zeros((128, 120), f)
    smalls[:, S_BU:S_BU + 16] = np.asarray(L0_attn_a_bu, f).reshape(16, 128).T
    smalls[:, S_BV:S_BV + 16] = np.asarray(L0_attn_a_bv, f).reshape(16, 128).T
    smalls[:, S_CB:S_CB + 16] = np.asarray(L0_attn_a_conv_b, f).reshape(16, 128).T
    smalls[:, S_BG2:S_BG2 + 16] = np.asarray(L0_attn_a_bg2, f).reshape(16, 128).T
    smalls[:, S_BG1:S_BG1 + 4] = np.asarray(L0_attn_a_bg1, f).reshape(4, 128).T
    cw = np.asarray(L0_attn_a_conv_w, f).reshape(2048, 3)
    smalls[:, S_CW:S_CW + 48] = cw.reshape(16, 128, 3).transpose(1, 0, 2).reshape(128, 48)

    onescol = np.ones((128, 2), f)
    onesrow = np.ones((1, 128), f)
    ones82 = np.ones((8, 2), f)

    xpad = np.zeros((B, L + 2, D), f)
    xpad[:, 1:L + 1] = x

    in_maps = []
    for c in range(NCORE):
        s, q = c // 4, c % 4
        xTc = np.zeros((COLS, D), f)
        xTc[1:TOK + 3] = xpad[s, q * TOK: q * TOK + TOK + 2, :]
        xT = np.ascontiguousarray(xTc.T)
        mask = np.zeros((8, 1), f)
        for r in range(NCORE):
            if r // 4 == s and r < c:
                mask[r, 0] = 1.0
        m = dict(packs)
        m.update({"xT": xT, "smalls": smalls, "onescol": onescol,
                  "onesrow": onesrow, "ones82": ones82, "mask8": mask})
        in_maps.append(m)

    if _NC_CACHE[0] is None:
        _NC_CACHE[0] = build_nc()
    nc = _NC_CACHE[0]
    res = run_bass_kernel_spmd(nc, in_maps, core_ids=list(range(NCORE)))

    y = np.empty((B, L, D), f)
    for c in range(NCORE):
        s, q = c // 4, c % 4
        y[s, q * TOK:(q + 1) * TOK, :] = res.results[c]["out"].T
    return y


# version 02
# speedup vs baseline: 2.7381x; 2.7381x over previous
import os
import numpy as np
from contextlib import ExitStack

import concourse.bass as bass
import concourse.mybir as mybir
from concourse.bass_utils import run_bass_kernel_spmd

F32 = mybir.dt.float32
F32R = mybir.dt.float32r
BF16 = mybir.dt.bfloat16
FP8 = mybir.dt.float8e4
U8 = mybir.dt.uint8
AF = mybir.ActivationFunctionType
ALU = mybir.AluOpType
AX = mybir.AxisListType.X

B, L, D, R, DH, HID = 2, 4096, 2048, 2048, 512, 8192
NCORE = 8
TOK = 1024
CH = 16
HCH = 4
JCH = 64
COLS = 1028
CB = COLS * 4
REGB = CH * CB
RL, RH = 2, 1026
DENOM = float(np.float64(np.sqrt(np.float32(D))) / L)
EPS = 1e-5

S_BU, S_BV, S_CB, S_BG2, S_BG1, S_CW = 0, 16, 32, 48, 64, 68

ENGINES = ("tensor", "vector", "scalar", "gpsimd", "sync")

N_PHASES = int(os.environ.get("KPHASES", "99"))
PARANOID = os.environ.get("KPAR", "0") == "1"
DMASER = os.environ.get("KDMASER", "1") == "1"


class Buf:
    __slots__ = ("name", "writer", "readers")

    def __init__(self, name="b", seed=()):
        self.name = name
        self.writer = None
        self.readers = list(seed)


class Prog:
    def __init__(self):
        self.counts = {e: 0 for e in ENGINES}
        self.marks = {e: {p: 0 for p in ENGINES} for e in ENGINES}
        self.ops = {e: [] for e in ENGINES}

    def emit(self, eng, fn, reads=(), writes=(), dma=False):
        need = {}
        for b in reads:
            if b.writer is not None:
                pe, pc = b.writer
                need[pe] = max(need.get(pe, 0), pc)
        for b in writes:
            if b.writer is not None:
                pe, pc = b.writer
                need[pe] = max(need.get(pe, 0), pc)
            for (pe, pc) in b.readers:
                need[pe] = max(need.get(pe, 0), pc)
        if PARANOID:
            for pe in ENGINES:
                if pe != eng:
                    need[pe] = max(need.get(pe, 0), self.counts[pe])
        if DMASER and dma:
            need[eng] = max(need.get(eng, 0), self.counts[eng])
        waits = []
        for pe in ENGINES:
            pc = need.get(pe, 0)
            if pc > self.marks[eng][pe]:
                waits.append((pe, pc))
                self.marks[eng][pe] = pc
        inc = 16 if dma else 1
        self.counts[eng] += inc
        self.ops[eng].append((fn, waits, inc))
        tok = (eng, self.counts[eng])
        for b in reads:
            b.readers.append(tok)
        for b in writes:
            b.writer = tok
            b.readers = []
        return tok

    def emit_group(self, eng, fns, reads=(), writes=()):
        need = {}
        for b in reads:
            if b.writer is not None:
                pe, pc = b.writer
                need[pe] = max(need.get(pe, 0), pc)
        for b in writes:
            if b.writer is not None:
                pe, pc = b.writer
                need[pe] = max(need.get(pe, 0), pc)
            for (pe, pc) in b.readers:
                need[pe] = max(need.get(pe, 0), pc)
        waits = []
        for pe in ENGINES:
            pc = need.get(pe, 0)
            if pc > self.marks[eng][pe]:
                waits.append((pe, pc))
                self.marks[eng][pe] = pc
        for i, fn in enumerate(fns):
            w = waits if i == 0 else []
            inc = 1 if i == len(fns) - 1 else 0
            self.ops[eng].append((fn, w, inc))
        self.counts[eng] += 1
        tok = (eng, self.counts[eng])
        for b in reads:
            b.readers.append(tok)
        for b in writes:
            b.writer = tok
            b.readers = []
        return tok

    def build(self, nc, sems):
        with nc.Block() as block:
            def mk(name):
                def run(engine):
                    for (fn, waits, inc) in self.ops[name]:
                        for (pe, pc) in waits:
                            engine.wait_ge(sems[pe], pc)
                        ins = fn(engine)
                        if inc:
                            ins.then_inc(sems[name], inc)
                return run
            block.tensor(mk("tensor"))
            block.vector(mk("vector"))
            block.scalar(mk("scalar"))
            block.gpsimd(mk("gpsimd"))
            block.sync(mk("sync"))


class Region:

    def __init__(self, base, base_esz=4):
        self.base = base
        self.base_esz = base_esz
        self.live = []
        self.floor = {}

    def fence(self):
        for b in self.live:
            toks = list(b.readers)
            if b.writer is not None:
                toks.append(b.writer)
            for (pe, pc) in toks:
                self.floor[pe] = max(self.floor.get(pe, 0), pc)
        self.live = []
        return [(pe, pc) for pe, pc in self.floor.items()]

    def buf(self, name, seed):
        b = Buf(name, seed)
        self.live.append(b)
        return b

    def carve(self, off, n, dt, seed, name="c"):
        esz = 4 if dt in (F32, F32R) else (2 if dt is BF16 else 1)
        b = self.base_esz
        assert off % b == 0 and (n * esz) % b == 0
        ap = self.base[:, off // b: off // b + (n * esz) // b]
        if ap.dtype is not dt:
            ap = ap.bitcast(dt)
        return ap, self.buf(name, seed)


def build_nc():
    nc = bass.Bass("TRN2", target_bir_lowering=False, debug=False,
                   dynamic_dma_scratch_size=8192)
    pr = Prog()
    ctx = ExitStack()

    d_in = {}
    for name, shape in [
        ("xT", [D, COLS]),
        ("wu_p", [CH, 128, 2048]), ("wv_p", [CH, 128, 2048]),
        ("wg1_p", [HCH, 128, 2048]), ("wg2_p", [CH, 128, 512]),
        ("wq_p", [CH, 128, 2048]), ("wk_p", [CH, 128, 2048]),
        ("wvb_p", [CH, 128, 2048]), ("wo_p", [CH, 128, 2048]),
        ("g0_p", [JCH, 128, 2048]), ("g1_p", [JCH, 128, 2048]),
        ("lin_p", [CH, 128, 8192]),
        ("smalls", [128, 120]), ("onescol", [128, 2]),
        ("onesrow", [1, 128]), ("ones82", [8, 2]), ("mask8", [8, 1]),
    ]:
        d_in[name] = nc.dram_tensor(name, shape, F32, kind="ExternalInput").ap()
    out_d = nc.dram_tensor("out", [D, TOK], F32, kind="ExternalOutput").ap()

    xsp_d = nc.dram_tensor("xsp", [128, REGB // 4], F32R, kind="Internal").ap()
    ugd_d = nc.dram_tensor("ugd", [R, TOK], F32R, kind="Internal").ap()
    qd_d = nc.dram_tensor("qd", [D, TOK], F32R, kind="Internal").ap()
    cc_in = [nc.dram_tensor(f"ccin{i}", [128, 16], F32, kind="Internal")
             for i in range(2)]
    cc_out = [nc.dram_tensor(f"ccout{i}", [1024, 16], F32, kind="Internal")
              for i in range(2)]

    rx_t = ctx.enter_context(nc.sbuf_tensor("rx", [128, REGB // 4], F32R))
    ry_t = ctx.enter_context(nc.sbuf_tensor("ry", [128, REGB // 4], F32R))
    c1_t = ctx.enter_context(nc.sbuf_tensor("c1", [128, 69632], FP8))
    c2_t = ctx.enter_context(nc.sbuf_tensor("c2", [128, 4096], F32R))
    RX, RY = Region(rx_t[:]), Region(ry_t[:])
    C1, C2 = Region(c1_t[:], base_esz=1), Region(c2_t[:])

    sm_t = ctx.enter_context(nc.sbuf_tensor("smalls_t", [128, 120], F32))
    oc_t = ctx.enter_context(nc.sbuf_tensor("onescol_t", [128, 2], F32R))
    orow_t = ctx.enter_context(nc.sbuf_tensor("onesrow_t", [1, 128], F32R))
    o82_t = ctx.enter_context(nc.sbuf_tensor("ones82_t", [8, 2], F32R))
    msk_t = ctx.enter_context(nc.sbuf_tensor("mask_t", [8, 1], F32))
    tot_t = ctx.enter_context(nc.sbuf_tensor("tot_t", [128, 16], F32))
    car_t = ctx.enter_context(nc.sbuf_tensor("car_t", [128, 32], F32))
    end_t = ctx.enter_context(nc.sbuf_tensor("end_t", [1, 2], F32))

    ps_t = ctx.enter_context(nc.psum_tensor("ps", [128, 4096], F32))
    PS = ps_t[:]

    sems = {e: ctx.enter_context(nc.semaphore(name=f"sem_{e}")) for e in ENGINES}

    xB = [RX.buf(f"x{c}", ()) for c in range(CH)]
    yB = [RY.buf(f"y{c}", ()) for c in range(CH)]
    smB, ocB, orB, o82B, mskB = Buf("sm"), Buf("oc"), Buf("or"), Buf("o82"), Buf("msk")
    totB, carB = Buf("tot"), Buf("car")
    xspB, ugdB, qdB = Buf("xsp"), Buf("ugd"), Buf("qd")
    ccB = [Buf("cc0"), Buf("cc1")]
    outB = Buf("outd")
    psB = [Buf(f"ps{i}") for i in range(4)]
    ps_off = [0, 1024, 2048, 3072]
    ps_rot = [0]

    def ps_next():
        i = ps_rot[0] % 4
        ps_rot[0] += 1
        return PS[:, ps_off[i]:ps_off[i] + 1024], psB[i]

    rxF, ryF = rx_t[:], ry_t[:]

    def xc(c):
        return rxF[:, c * COLS:(c + 1) * COLS]

    def yc(c):
        return ryF[:, c * COLS:(c + 1) * COLS]

    E = pr.emit

    E("scalar", lambda e: e.dma_start(out=sm_t[:], in_=d_in["smalls"]), writes=[smB], dma=True)
    E("scalar", lambda e: e.dma_start(out=oc_t[:], in_=d_in["onescol"].bitcast(F32R)), writes=[ocB], dma=True)
    E("scalar", lambda e: e.dma_start(out=orow_t[:], in_=d_in["onesrow"].bitcast(F32R)), writes=[orB], dma=True)
    E("scalar", lambda e: e.dma_start(out=o82_t[:], in_=d_in["ones82"].bitcast(F32R)), writes=[o82B], dma=True)
    E("scalar", lambda e: e.dma_start(out=msk_t[:], in_=d_in["mask8"]), writes=[mskB], dma=True)
    for c in range(CH):
        E("scalar", lambda e, c=c: e.dma_start(
            out=xc(c), in_=d_in["xT"][c * 128:(c + 1) * 128, :].bitcast(F32R)),
          writes=[xB[c]], dma=True)

    def ln():
        seed = C2.fence()
        rowm = C2.carve(0, COLS, F32R, seed, "rowm")
        rowv = C2.carve(4112, COLS, F32R, seed, "rowv")
        rowr = C2.carve(8224, COLS, F32R, seed, "rowr")
        rm, rv, rr = rowm[0][0:1, :], rowv[0][0:1, :], rowr[0][0:1, :]
        yseed = RY.fence()
        for c in range(CH):
            yB[c] = RY.buf(f"y{c}", yseed)
        splits = [(0, 512), (512, 512), (1024, 4)]
        for c in range(CH):
            E("scalar", lambda e, c=c: e.activation(yc(c), xc(c), AF.Square),
              reads=[xB[c]], writes=[yB[c]])
            fns = []
            for (o, n) in splits:
                fns.append(lambda e, c=c, o=o, n=n: e.matmul(
                    PS[0:1, o:o + n], oc_t[:][:, 0:1], xc(c)[:, o:o + n],
                    start=(c == 0), stop=(c == CH - 1)))
                fns.append(lambda e, c=c, o=o, n=n: e.matmul(
                    PS[0:1, 2048 + o:2048 + o + n], oc_t[:][:, 0:1], yc(c)[:, o:o + n],
                    start=(c == 0), stop=(c == CH - 1)))
            pr.emit_group("tensor", fns, reads=[xB[c], yB[c], ocB],
                          writes=[psB[0], psB[1], psB[2], psB[3]])
        E("vector", lambda e, a=rm: e.tensor_scalar_mul(a, PS[0:1, 0:COLS], 1.0 / D),
          reads=[psB[0], psB[1]], writes=[rowm[1]])
        E("vector", lambda e, a=rv: e.tensor_scalar_mul(a, PS[0:1, 2048:2048 + COLS], 1.0 / D),
          reads=[psB[2], psB[3]], writes=[rowv[1]])
        E("vector", lambda e, a=rr, m=rm: e.tensor_mul(a, m, m),
          reads=[rowm[1]], writes=[rowr[1]])
        E("vector", lambda e, a=rv, m2=rr: e.tensor_sub(a, a, m2),
          reads=[rowr[1], rowv[1]], writes=[rowv[1]])
        E("vector", lambda e, a=rv: e.tensor_scalar_add(a, a, EPS),
          reads=[rowv[1]], writes=[rowv[1]])
        E("scalar", lambda e, a=rv: e.activation(a, a, AF.Sqrt),
          reads=[rowv[1]], writes=[rowv[1]])
        E("vector", lambda e, a=rr, v=rv: e.reciprocal(a, v),
          reads=[rowv[1]], writes=[rowr[1]])
        E("vector", lambda e, a=rm, r=rr: e.scalar_tensor_tensor(
            a, a, -1.0, r, ALU.mult, ALU.mult),
          reads=[rowr[1], rowm[1]], writes=[rowm[1]])
        for (o, n) in splits:
            E("tensor", lambda e, o=o, n=n, r=rr: e.matmul(
                PS[:, o:o + n], orow_t[:], r[0:1, o:o + n], start=True, stop=True),
              reads=[rowr[1], orB], writes=[psB[0] if o < 1024 else psB[1]])
            E("tensor", lambda e, o=o, n=n, r=rm: e.matmul(
                PS[:, 2048 + o:2048 + o + n], orow_t[:], r[0:1, o:o + n], start=True, stop=True),
              reads=[rowm[1], orB], writes=[psB[2] if o < 1024 else psB[3]])
        for c in range(CH):
            E("vector", lambda e, c=c: e.tensor_mul(yc(c), xc(c), PS[:, 0:COLS]),
              reads=[xB[c], psB[0], psB[1]], writes=[yB[c]])
            E("vector", lambda e, c=c: e.tensor_add(yc(c), yc(c), PS[:, 2048:2048 + COLS]),
              reads=[psB[2], psB[3], yB[c]], writes=[yB[c]])

    def spill_x():
        E("scalar", lambda e: e.dma_start(out=xsp_d, in_=rx_t[:]),
          reads=list(xB), writes=[xspB], dma=True)

    def restore_x(seed):
        for c in range(CH):
            xB[c] = RX.buf(f"x{c}", seed)
            E("scalar", lambda e, c=c: e.dma_start(
                out=xc(c), in_=xsp_d[:, c * COLS:(c + 1) * COLS]),
              reads=[xspB], writes=[xB[c]], dma=True)

    def carry_exchange(idx):
        cin, cout = cc_in[idx].ap(), cc_out[idx].ap()
        gma, gmb = RY.carve(4096 * 4, 2048, F32R, RY.fence(), "gm")
        gm = gma[0:8, :]
        E("scalar", lambda e, cin=cin: e.dma_start(out=cin, in_=tot_t[:]),
          reads=[totB], writes=[ccB[idx]], dma=True)
        E("gpsimd", lambda e, i=idx: e.collective_compute(
            "AllGather", ALU.bypass, [list(range(NCORE))],
            [cc_in[i].ap().opt()], [cc_out[i].ap().opt()]),
          reads=[ccB[idx]], writes=[ccB[idx]])
        E("scalar", lambda e, cout=cout, gm=gm: e.dma_start(
            out=gm, in_=cout.rearrange("(r p) c -> r (p c)", p=128).bitcast(F32R)),
          reads=[ccB[idx]], writes=[gmb], dma=True)
        E("vector", lambda e, gm=gm: e.tensor_scalar_mul(gm, gm, msk_t[:][:, 0:1]),
          reads=[mskB, gmb], writes=[gmb])
        gmr = gm.rearrange("r (p c) -> r c p", c=16)
        for m in range(CH):
            E("tensor", lambda e, m=m, g=gmr: e.matmul(
                PS[:, 3072 + 2 * m:3072 + 2 * m + 2], g[:, m, :], o82_t[:],
                start=True, stop=True),
              reads=[gmb, o82B], writes=[psB[3]])
        E("vector", lambda e: e.tensor_copy(car_t[:], PS[:, 3072:3104]),
          reads=[psB[3]], writes=[carB])

    def mm_acc(lhs_ap_fn, lhs_buf, rhs_fn, rhs_buf_fn, nk):
        psap, psb = ps_next()
        fns = []
        for k in range(nk):
            for h in range(2):
                fns.append(lambda e, k=k, h=h, p=psap: e.matmul(
                    p[:, h * 512:(h + 1) * 512], lhs_ap_fn(k), rhs_fn(k, h),
                    start=(k == 0), stop=(k == nk - 1)))
        pr.emit_group("tensor", fns,
                      reads=[lhs_buf] + [rhs_buf_fn(k) for k in range(nk)],
                      writes=[psb])
        return psap, psb

    def y_rhs(k, h):
        return yc(k)[:, RL + h * 512: RL + (h + 1) * 512]

    def y_buf(k):
        return yB[k]

    ph = [0]

    def phase(f):
        if ph[0] < N_PHASES:
            f()
        ph[0] += 1

    phase(ln)

    def p_conv():
        seed = C1.fence()
        ct = [C1.carve(45056, TOK, F32, seed, "cv0"),
              C1.carve(49152, TOK, F32, seed, "cv1")]
        for c in range(CH):
            a, ab = ct[c % 2]
            cw0 = sm_t[:][:, S_CW + 3 * c: S_CW + 3 * c + 1]
            cw1 = sm_t[:][:, S_CW + 3 * c + 1: S_CW + 3 * c + 2]
            cw2 = sm_t[:][:, S_CW + 3 * c + 2: S_CW + 3 * c + 3]
            cbv = sm_t[:][:, S_CB + c: S_CB + c + 1]
            E("vector", lambda e, c=c, a=a, w=cw0, bv=cbv: e.tensor_scalar(
                a, yc(c)[:, 1:1025], w, bv, ALU.mult, ALU.add),
              reads=[yB[c], smB], writes=[ab])
            E("vector", lambda e, c=c, a=a, w=cw1: e.scalar_tensor_tensor(
                a, yc(c)[:, 2:1026], w, a, ALU.mult, ALU.add),
              reads=[yB[c], smB, ab], writes=[ab])
            E("vector", lambda e, c=c, a=a, w=cw2: e.scalar_tensor_tensor(
                a, yc(c)[:, 3:1027], w, a, ALU.mult, ALU.add),
              reads=[yB[c], smB, ab], writes=[ab])
            E("vector", lambda e, c=c, a=a: e.tensor_add(
                xc(c)[:, RL:RH], xc(c)[:, RL:RH], a),
              reads=[ab, xB[c]], writes=[xB[c]])
    phase(p_conv)

    def p_attn_a_proj():
        spill_x()
        seed = RX.fence()
        wsA = [RX.carve(0, 2048, F32R, seed, "wsA0"),
               RX.carve(8192, 2048, F32R, seed, "wsA1")]
        wsB = [RX.carve(16384, 2048, F32R, seed, "wsB0"),
               RX.carve(24576, 2048, F32R, seed, "wsB1")]
        wg2f = [RX.carve(32768, 512, F32R, seed, "wg2f0"),
                RX.carve(34816, 512, F32R, seed, "wg2f1")]
        gtmp = RX.carve(36864, TOK, F32R, seed, "gtmp")
        ugt = [RX.carve(40960, TOK, F32R, seed, "ugt0"),
               RX.carve(45056, TOK, F32R, seed, "ugt1")]
        cseed = C1.fence()
        vgc = [C1.carve(c * 2048, TOK, BF16, cseed, f"vg{c}") for c in range(CH)]
        hc = [C1.carve(32768 + k * 2048, TOK, BF16, cseed, f"h{k}") for k in range(HCH)]
        wg2b = [C1.carve(40960, 512, BF16, cseed, "wg2b0"),
                C1.carve(41984, 512, BF16, cseed, "wg2b1")]
        for m in range(HCH):
            buf = wsA[m % 2]
            E("sync", lambda e, m=m, a=buf[0]: e.dma_start(
                out=a, in_=d_in["wg1_p"][m].bitcast(F32R)), writes=[buf[1]], dma=True)
            psap, psb = mm_acc(
                lambda k, a=buf[0]: a[:, k * 128:(k + 1) * 128], buf[1],
                y_rhs, y_buf, CH)
            E("scalar", lambda e, m=m, p=psap, a=hc[m][0]: e.activation(
                a, p, AF.Relu, bias=sm_t[:][:, S_BG1 + m:S_BG1 + m + 1]),
              reads=[psb, smB], writes=[hc[m][1]])
        for m in range(CH):
            bufu, bufv = wsA[m % 2], wsB[m % 2]
            bufg, bufgb = wg2f[m % 2], wg2b[m % 2]
            E("sync", lambda e, m=m, a=bufg[0]: e.dma_start(
                out=a, in_=d_in["wg2_p"][m].bitcast(F32R)), writes=[bufg[1]], dma=True)
            E("sync", lambda e, m=m, a=bufu[0]: e.dma_start(
                out=a, in_=d_in["wu_p"][m].bitcast(F32R)), writes=[bufu[1]], dma=True)
            E("sync", lambda e, m=m, a=bufv[0]: e.dma_start(
                out=a, in_=d_in["wv_p"][m].bitcast(F32R)), writes=[bufv[1]], dma=True)
            E("scalar", lambda e, a=bufgb[0], f=bufg[0]: e.copy(a, f),
              reads=[bufg[1]], writes=[bufgb[1]])
            psg, psgb = mm_acc(
                lambda k, a=bufgb[0]: a[:, k * 128:(k + 1) * 128], bufgb[1],
                lambda k, h: hc[k][0][:, h * 512:(h + 1) * 512],
                lambda k: hc[k][1], HCH)
            E("scalar", lambda e, a=gtmp[0], p=psg, m=m: e.activation(
                a, p, AF.Sigmoid, bias=sm_t[:][:, S_BG2 + m:S_BG2 + m + 1]),
              reads=[psgb, smB], writes=[gtmp[1]])
            psu, psub = mm_acc(
                lambda k, a=bufu[0]: a[:, k * 128:(k + 1) * 128], bufu[1],
                y_rhs, y_buf, CH)
            ug = ugt[m % 2]
            E("vector", lambda e, a=ug[0], p=psu, m=m, g=gtmp[0]: e.scalar_tensor_tensor(
                a, p, sm_t[:][:, S_BU + m:S_BU + m + 1], g, ALU.add, ALU.mult),
              reads=[psub, smB, gtmp[1]], writes=[ug[1]])
            E("vector", lambda e, a=ug[0], m=m: e.reduce_sum(
                tot_t[:][:, m:m + 1], a, axis=AX),
              reads=[ug[1]], writes=[totB])
            E("scalar", lambda e, a=ug[0], m=m: e.dma_start(
                out=ugd_d[m * 128:(m + 1) * 128, :], in_=a),
              reads=[ug[1]], writes=[ugdB], dma=True)
            psv, psvb = mm_acc(
                lambda k, a=bufv[0]: a[:, k * 128:(k + 1) * 128], bufv[1],
                y_rhs, y_buf, CH)
            E("vector", lambda e, m=m, p=psv, g=gtmp[0], a=vgc[m][0]: e.scalar_tensor_tensor(
                a, p, sm_t[:][:, S_BV + m:S_BV + m + 1], g, ALU.add, ALU.mult),
              reads=[psvb, smB, gtmp[1]], writes=[vgc[m][1]])
        return vgc
    vgc_holder = []
    phase(lambda: vgc_holder.append(p_attn_a_proj()))

    gc_holder = []

    def p_attn_a_scan():
        vgc = vgc_holder[0]
        carry_exchange(0)
        cseed = C1.fence()
        ugr = [C1.carve(45056, TOK, F32R, cseed, "ugr0"),
               C1.carve(49152, TOK, F32R, cseed, "ugr1")]
        ktmp = C1.carve(53248, TOK, F32R, cseed, "ktmp")
        yseed = RY.fence()
        gc = [RY.carve(c * 4096, TOK, F32R, yseed, f"g{c}") for c in range(CH)]
        gc_holder.append(gc)
        for m in range(CH):
            u = ugr[m % 2]
            E("scalar", lambda e, a=u[0], m=m: e.dma_start(
                out=a, in_=ugd_d[m * 128:(m + 1) * 128, :]),
              reads=[ugdB], writes=[u[1]], dma=True)
            E("vector", lambda e, a=ktmp[0], u=u[0], m=m: e.tensor_tensor_scan(
                a, u, u, car_t[:][:, 2 * m:2 * m + 1], ALU.add, ALU.bypass),
              reads=[u[1], carB], writes=[ktmp[1]])
            E("vector", lambda e, m=m, a=ktmp[0], g=gc[m][0], v=vgc[m][0]: e.tensor_mul(
                g, a, v),
              reads=[ktmp[1], vgc[m][1]], writes=[gc[m][1]])
    phase(p_attn_a_scan)

    def p_attn_a_uv():
        kuv = os.environ.get("KUV", "all")
        gc = gc_holder[0]
        seed = C2.fence()
        ws = [C2.carve(0, 2048, F32R, seed, "uvw0"),
              C2.carve(8192, 2048, F32R, seed, "uvw1")]
        if kuv != "nores":
            restore_x(RX.fence())
        for mo in range(CH):
            if kuv == "restonly":
                continue
            buf = ws[mo % 2]
            E("sync", lambda e, mo=mo, a=buf[0]: e.dma_start(
                out=a, in_=d_in["wu_p"][mo].bitcast(F32R)), writes=[buf[1]], dma=True)
            psap, psb = mm_acc(
                lambda k, a=buf[0]: a[:, k * 128:(k + 1) * 128], buf[1],
                lambda k, h: gc[k][0][:, h * 512:(h + 1) * 512],
                lambda k: gc[k][1], CH)
            if kuv == "all":
                E("vector", lambda e, mo=mo, p=psap: e.tensor_add(
                    xc(mo)[:, RL:RH], xc(mo)[:, RL:RH], p),
                  reads=[psb, xB[mo]], writes=[xB[mo]])
    phase(p_attn_a_uv)

    def p_ffn(final):
        kffn = os.environ.get("KFFN", "all") if final else "all"
        if kffn in ("xpre", "xln", "xspill", "xdump"):
            if kffn == "xln":
                ln()
            if kffn == "xspill":
                spill_x()
                restore_x(RX.fence())
            if kffn == "xdump":
                spill_x()
                for mo in range(CH):
                    E("sync", lambda e, mo=mo: e.dma_start(
                        out=out_d[mo * 128:(mo + 1) * 128, :].bitcast(F32R),
                        in_=xsp_d[:, mo * COLS + RL: mo * COLS + RH]),
                      reads=[xspB], writes=[outB], dma=True)
                E("vector", lambda e: e.memset(end_t[0:1, 0:1], 0.0), reads=[outB])
                return
            yseed = RY.fence()
            otl = [RY.carve(0, TOK, F32R, yseed, "ot0"),
                   RY.carve(4096, TOK, F32R, yseed, "ot1")]
            for mo in range(CH):
                o = otl[mo % 2]
                E("vector", lambda e, mo=mo, a=o[0]: e.tensor_scalar_mul(
                    a, xc(mo)[:, RL:RH], 1.0),
                  reads=[xB[mo]], writes=[o[1]])
                E("sync", lambda e, mo=mo, a=o[0]: e.dma_start(
                    out=out_d[mo * 128:(mo + 1) * 128, :].bitcast(F32R), in_=a),
                  reads=[o[1]], writes=[outB], dma=True)
            return
        ln()
        spill_x()
        xseed = RX.fence()
        g0 = [RX.carve(0, 2048, F32R, xseed, "g0a"),
              RX.carve(8192, 2048, F32R, xseed, "g0b")]
        g1 = [RX.carve(16384, 2048, F32R, xseed, "g1a"),
              RX.carve(24576, 2048, F32R, xseed, "g1b")]
        gel = RX.carve(32768, TOK, F32R, xseed, "gel")
        cseed = C1.fence()
        a8 = [C1.carve(j * 1024, TOK, FP8, cseed, f"a8{j}") for j in range(JCH)]
        for j in range(JCH if kffn != "xonly" else 0):
            b0, b1 = g0[j % 2], g1[j % 2]
            E("sync", lambda e, j=j, a=b0[0]: e.dma_start(
                out=a, in_=d_in["g0_p"][j].bitcast(F32R)), writes=[b0[1]], dma=True)
            E("sync", lambda e, j=j, a=b1[0]: e.dma_start(
                out=a, in_=d_in["g1_p"][j].bitcast(F32R)), writes=[b1[1]], dma=True)
            psg, psgb = mm_acc(
                lambda k, a=b0[0]: a[:, k * 128:(k + 1) * 128], b0[1],
                y_rhs, y_buf, CH)
            E("scalar", lambda e, a=gel[0], p=psg: e.activation(a, p, AF.Gelu_apprx_tanh),
              reads=[psgb], writes=[gel[1]])
            psf, psfb = mm_acc(
                lambda k, a=b1[0]: a[:, k * 128:(k + 1) * 128], b1[1],
                y_rhs, y_buf, CH)
            E("vector", lambda e, j=j, a=gel[0], p=psf, o=a8[j][0]: e.tensor_mul(o, a, p),
              reads=[gel[1], psfb], writes=[a8[j][1]])
        c2seed = C2.fence()
        vfq = C2.carve(0, 2048, F32R, c2seed, "vfq")
        v8 = [C1.carve(65536, 2048, FP8, C1.fence(), "v80"),
              C1.carve(67584, 2048, FP8, (), "v81")]
        otl = None
        if final:
            yseed = RY.fence()
            otl = [RY.carve(0, TOK, F32R, yseed, "ot0"),
                   RY.carve(4096, TOK, F32R, yseed, "ot1")]
        restore_x(RX.fence())
        if kffn == "xonly":
            for mo in range(CH):
                o = otl[mo % 2]
                E("vector", lambda e, mo=mo, a=o[0]: e.tensor_scalar_mul(
                    a, xc(mo)[:, RL:RH], 1.0),
                  reads=[xB[mo]], writes=[o[1]])
                E("sync", lambda e, mo=mo, a=o[0]: e.dma_start(
                    out=out_d[mo * 128:(mo + 1) * 128, :].bitcast(F32R), in_=a),
                  reads=[o[1]], writes=[outB], dma=True)
            return
        for mo in range(CH):
            psap, psb = ps_next()
            for qq in range(4):
                q8 = v8[qq % 2]
                E("sync", lambda e, mo=mo, qq=qq, a=vfq[0]: e.dma_start(
                    out=a, in_=d_in["lin_p"][mo][:, qq * 2048:(qq + 1) * 2048].bitcast(F32R)),
                  reads=[], writes=[vfq[1]], dma=True)
                E("scalar", lambda e, a=q8[0], f=vfq[0]: e.copy(a, f),
                  reads=[vfq[1]], writes=[q8[1]])
                fns = []
                abufs = []
                for jj in range(16):
                    j = qq * 16 + jj
                    abufs.append(a8[j][1])
                    for h in range(2):
                        fns.append(lambda e, j=j, jj=jj, h=h, q=q8[0], p=psap: e.matmul(
                            p[:, h * 512:(h + 1) * 512], q[:, jj * 128:(jj + 1) * 128],
                            a8[j][0][:, h * 512:(h + 1) * 512],
                            start=(j == 0), stop=(j == JCH - 1)))
                pr.emit_group("tensor", fns, reads=[q8[1]] + abufs, writes=[psb])
            if final:
                o = otl[mo % 2]
                E("vector", lambda e, mo=mo, p=psap, a=o[0]: e.tensor_add(
                    a, xc(mo)[:, RL:RH], p),
                  reads=[psb, xB[mo]], writes=[o[1]])
                E("sync", lambda e, mo=mo, a=o[0]: e.dma_start(
                    out=out_d[mo * 128:(mo + 1) * 128, :].bitcast(F32R), in_=a),
                  reads=[o[1]], writes=[outB], dma=True)
            else:
                E("vector", lambda e, mo=mo, p=psap: e.tensor_add(
                    xc(mo)[:, RL:RH], xc(mo)[:, RL:RH], p),
                  reads=[psb, xB[mo]], writes=[xB[mo]])
    phase(lambda: p_ffn(False))

    def p_attn_b():
        katb = os.environ.get("KATB", "all")
        ln()
        spill_x()
        xseed = RX.fence()
        wq = [RX.carve(0, 2048, F32R, xseed, "wq0"),
              RX.carve(8192, 2048, F32R, xseed, "wq1")]
        wk = [RX.carve(16384, 2048, F32R, xseed, "wk0"),
              RX.carve(24576, 2048, F32R, xseed, "wk1")]
        wv = [RX.carve(32768, 2048, F32R, xseed, "wv0"),
              RX.carve(40960, 2048, F32R, xseed, "wv1")]
        ktm = RX.carve(49152, TOK, F32R, xseed, "ktm")
        qtm = [RX.carve(53248, TOK, F32R, xseed, "qt0"),
               RX.carve(57344, TOK, F32R, xseed, "qt1")]
        cseed = C1.fence()
        kvc = [C1.carve(c * 4096, TOK, F32, cseed, f"kv{c}") for c in range(CH)]
        if katb == "ln":
            return
        nM = int(katb[1:]) if katb.startswith("m") else CH
        for m in range(nM):
            bq, bk, bv = wq[m % 2], wk[m % 2], wv[m % 2]
            E("sync", lambda e, m=m, a=bq[0]: e.dma_start(
                out=a, in_=d_in["wq_p"][m].bitcast(F32R)), writes=[bq[1]], dma=True)
            E("sync", lambda e, m=m, a=bk[0]: e.dma_start(
                out=a, in_=d_in["wk_p"][m].bitcast(F32R)), writes=[bk[1]], dma=True)
            E("sync", lambda e, m=m, a=bv[0]: e.dma_start(
                out=a, in_=d_in["wvb_p"][m].bitcast(F32R)), writes=[bv[1]], dma=True)
            if katb != "noq":
                psq, psqb = mm_acc(
                    lambda k, a=bq[0]: a[:, k * 128:(k + 1) * 128], bq[1],
                    y_rhs, y_buf, CH)
                if katb != "qne":
                    qb = qtm[m % 2]
                    E("scalar", lambda e, a=qb[0], p=psq: e.activation(a, p, AF.Copy, scale=DENOM),
                      reads=[psqb], writes=[qb[1]])
                    if katb != "qns":
                        E("sync", lambda e, a=qb[0], m=m: e.dma_start(
                            out=qd_d[m * 128:(m + 1) * 128, :], in_=a),
                          reads=[qb[1]], writes=[qdB], dma=True)
            psk, pskb = mm_acc(
                lambda k, a=bk[0]: a[:, k * 128:(k + 1) * 128], bk[1],
                y_rhs, y_buf, CH)
            E("vector", lambda e, a=ktm[0], p=psk: e.tensor_copy(a, p),
              reads=[pskb], writes=[ktm[1]])
            psv, psvb = mm_acc(
                lambda k, a=bv[0]: a[:, k * 128:(k + 1) * 128], bv[1],
                y_rhs, y_buf, CH)
            E("vector", lambda e, m=m, a=ktm[0], p=psv, o=kvc[m][0]: e.tensor_mul(o, a, p),
              reads=[ktm[1], psvb], writes=[kvc[m][1]])
            E("vector", lambda e, m=m, o=kvc[m][0]: e.reduce_sum(
                tot_t[:][:, m:m + 1], o, axis=AX),
              reads=[kvc[m][1]], writes=[totB])
        if katb == "proj":
            return
        carry_exchange(1)
        if katb == "cc":
            return
        yseed = RY.fence()
        vh = [RY.carve(c * 4096, TOK, F32R, yseed, f"vh{c}") for c in range(CH)]
        for m in range(CH):
            qb = qtm[m % 2]
            E("scalar", lambda e, a=qb[0], m=m: e.dma_start(
                out=a, in_=qd_d[m * 128:(m + 1) * 128, :]),
              reads=[qdB], writes=[qb[1]], dma=True)
            E("vector", lambda e, a=ktm[0], m=m, u=kvc[m][0]: e.tensor_tensor_scan(
                a, u, u, car_t[:][:, 2 * m:2 * m + 1], ALU.add, ALU.bypass),
              reads=[kvc[m][1], carB], writes=[ktm[1]])
            E("vector", lambda e, m=m, q=qb[0], a=ktm[0], o=vh[m][0]: e.tensor_mul(o, q, a),
              reads=[qb[1], ktm[1]], writes=[vh[m][1]])
        if katb == "scan":
            return
        c2seed = C2.fence()
        wo = [C2.carve(0, 2048, F32R, c2seed, "wo0"),
              C2.carve(8192, 2048, F32R, c2seed, "wo1")]
        restore_x(RX.fence())
        for mo in range(CH):
            buf = wo[mo % 2]
            E("sync", lambda e, mo=mo, a=buf[0]: e.dma_start(
                out=a, in_=d_in["wo_p"][mo].bitcast(F32R)), writes=[buf[1]], dma=True)
            psap, psb = mm_acc(
                lambda k, a=buf[0]: a[:, k * 128:(k + 1) * 128], buf[1],
                lambda k, h: vh[k][0][:, h * 512:(h + 1) * 512],
                lambda k: vh[k][1], CH)
            E("vector", lambda e, mo=mo, p=psap: e.tensor_add(
                xc(mo)[:, RL:RH], xc(mo)[:, RL:RH], p),
              reads=[psb, xB[mo]], writes=[xB[mo]])
    phase(p_attn_b)

    phase(lambda: p_ffn(True))

    if N_PHASES < 8:
        yseed = RY.fence()
        ot = [RY.carve(0, TOK, F32R, yseed, "dbg0"),
              RY.carve(4096, TOK, F32R, yseed, "dbg1")]
        for mo in range(CH):
            o = ot[mo % 2]
            E("vector", lambda e, mo=mo, a=o[0]: e.tensor_scalar_mul(
                a, xc(mo)[:, RL:RH], 1.0),
              reads=[xB[mo]], writes=[o[1]])
            E("sync", lambda e, mo=mo, a=o[0]: e.dma_start(
                out=out_d[mo * 128:(mo + 1) * 128, :].bitcast(F32R), in_=a),
              reads=[o[1]], writes=[outB], dma=True)

    fin = [(e, pr.counts[e]) for e in ENGINES if e != "vector" and pr.counts[e] > 0]
    endB = Buf("end")
    E("vector", lambda e: e.memset(end_t[:], 0.0), writes=[endB])
    pr.ops["vector"].append((lambda e: e.memset(end_t[:], 1.0), fin, 1))
    pr.counts["vector"] += 1

    with nc.allow_low_precision(reason="deliberate f32r/bf16/fp8 stores; f32 PSUM accumulation throughout"):
        pr.build(nc, sems)
    ctx.close()
    return nc


def _pack2d(A):
    K, M = A.shape
    return np.ascontiguousarray(
        A.reshape(K // 128, 128, M // 128, 128).transpose(2, 1, 0, 3)
        .reshape(M // 128, 128, K))


_NC_CACHE = [None]


def kernel(L0_attn_a_x, L0_attn_a_Wu, L0_attn_a_bu, L0_attn_a_Wv, L0_attn_a_bv,
           L0_attn_a_Wg1, L0_attn_a_bg1, L0_attn_a_Wg2, L0_attn_a_bg2,
           L0_attn_a_conv_w, L0_attn_a_conv_b, L0_attn_b_Wq, L0_attn_b_Wk,
           L0_attn_b_Wv, L0_attn_b_Wo, L0_ffn_w_gating, L0_ffn_w_linear):
    f = np.float32
    x = np.asarray(L0_attn_a_x, f)
    Wu = np.asarray(L0_attn_a_Wu, f); Wv = np.asarray(L0_attn_a_Wv, f)
    Wg1 = np.asarray(L0_attn_a_Wg1, f); Wg2 = np.asarray(L0_attn_a_Wg2, f)
    Wq = np.asarray(L0_attn_b_Wq, f); Wk = np.asarray(L0_attn_b_Wk, f)
    Wvb = np.asarray(L0_attn_b_Wv, f); Wo = np.asarray(L0_attn_b_Wo, f)
    wg = np.asarray(L0_ffn_w_gating, f); wl = np.asarray(L0_ffn_w_linear, f)

    packs = {
        "wu_p": _pack2d(np.ascontiguousarray(Wu.T)),
        "wv_p": _pack2d(np.ascontiguousarray(Wv.T)),
        "wg1_p": _pack2d(np.ascontiguousarray(Wg1.T)),
        "wg2_p": _pack2d(np.ascontiguousarray(Wg2.T)),
        "wq_p": _pack2d(np.ascontiguousarray(Wq.T)),
        "wk_p": _pack2d(np.ascontiguousarray(Wk.T)),
        "wvb_p": _pack2d(np.ascontiguousarray(Wvb.T)),
        "wo_p": _pack2d(np.ascontiguousarray(Wo.T)),
        "g0_p": _pack2d(wg[0]),
        "g1_p": _pack2d(wg[1]),
        "lin_p": _pack2d(wl),
    }
    smalls = np.zeros((128, 120), f)
    smalls[:, S_BU:S_BU + 16] = np.asarray(L0_attn_a_bu, f).reshape(16, 128).T
    smalls[:, S_BV:S_BV + 16] = np.asarray(L0_attn_a_bv, f).reshape(16, 128).T
    smalls[:, S_CB:S_CB + 16] = np.asarray(L0_attn_a_conv_b, f).reshape(16, 128).T
    smalls[:, S_BG2:S_BG2 + 16] = np.asarray(L0_attn_a_bg2, f).reshape(16, 128).T
    smalls[:, S_BG1:S_BG1 + 4] = np.asarray(L0_attn_a_bg1, f).reshape(4, 128).T
    cw = np.asarray(L0_attn_a_conv_w, f).reshape(2048, 3)
    smalls[:, S_CW:S_CW + 48] = cw.reshape(16, 128, 3).transpose(1, 0, 2).reshape(128, 48)

    onescol = np.ones((128, 2), f)
    onesrow = np.ones((1, 128), f)
    ones82 = np.ones((8, 2), f)

    xpad = np.zeros((B, L + 2, D), f)
    xpad[:, 1:L + 1] = x

    in_maps = []
    for c in range(NCORE):
        s, q = c // 4, c % 4
        xTc = np.zeros((COLS, D), f)
        xTc[1:TOK + 3] = xpad[s, q * TOK: q * TOK + TOK + 2, :]
        xT = np.ascontiguousarray(xTc.T)
        mask = np.zeros((8, 1), f)
        for r in range(NCORE):
            if r // 4 == s and r < c:
                mask[r, 0] = 1.0
        m = dict(packs)
        m.update({"xT": xT, "smalls": smalls, "onescol": onescol,
                  "onesrow": onesrow, "ones82": ones82, "mask8": mask})
        in_maps.append(m)

    if _NC_CACHE[0] is None:
        _NC_CACHE[0] = build_nc()
    nc = _NC_CACHE[0]
    res = run_bass_kernel_spmd(nc, in_maps, core_ids=list(range(NCORE)))

    y = np.empty((B, L, D), f)
    for c in range(NCORE):
        s, q = c // 4, c % 4
        y[s, q * TOK:(q + 1) * TOK, :] = res.results[c]["out"].T
    return y


# version 03
# speedup vs baseline: 2.8719x; 1.0489x over previous
import os
import numpy as np
from contextlib import ExitStack

import concourse.bass as bass
import concourse.mybir as mybir
from concourse.bass_utils import run_bass_kernel_spmd

F32 = mybir.dt.float32
F32R = mybir.dt.float32r
BF16 = mybir.dt.bfloat16
FP8 = mybir.dt.float8e4
U8 = mybir.dt.uint8
AF = mybir.ActivationFunctionType
ALU = mybir.AluOpType
AX = mybir.AxisListType.X

B, L, D, R, DH, HID = 2, 4096, 2048, 2048, 512, 8192
NCORE = 8
TOK = 1024
CH = 16
HCH = 4
JCH = 64
COLS = 1028
CB = COLS * 4
REGB = CH * CB
RL, RH = 2, 1026
DENOM = float(np.float64(np.sqrt(np.float32(D))) / L)
EPS = 1e-5

S_BU, S_BV, S_CB, S_BG2, S_BG1, S_CW = 0, 16, 32, 48, 64, 68

ENGINES = ("tensor", "vector", "scalar", "gpsimd", "sync")

N_PHASES = int(os.environ.get("KPHASES", "99"))
PARANOID = os.environ.get("KPAR", "0") == "1"
DMASER = os.environ.get("KDMASER", "1") == "1"


class Buf:
    __slots__ = ("name", "writer", "readers")

    def __init__(self, name="b", seed=()):
        self.name = name
        self.writer = None
        self.readers = list(seed)


class Prog:
    def __init__(self):
        self.counts = {e: 0 for e in ENGINES}
        self.marks = {e: {p: 0 for p in ENGINES} for e in ENGINES}
        self.ops = {e: [] for e in ENGINES}

    def emit(self, eng, fn, reads=(), writes=(), dma=False):
        need = {}
        for b in reads:
            if b.writer is not None:
                pe, pc = b.writer
                need[pe] = max(need.get(pe, 0), pc)
        for b in writes:
            if b.writer is not None:
                pe, pc = b.writer
                need[pe] = max(need.get(pe, 0), pc)
            for (pe, pc) in b.readers:
                need[pe] = max(need.get(pe, 0), pc)
        if PARANOID:
            for pe in ENGINES:
                if pe != eng:
                    need[pe] = max(need.get(pe, 0), self.counts[pe])
        if DMASER and dma:
            need[eng] = max(need.get(eng, 0), self.counts[eng])
        waits = []
        for pe in ENGINES:
            pc = need.get(pe, 0)
            if pc > self.marks[eng][pe]:
                waits.append((pe, pc))
                self.marks[eng][pe] = pc
        inc = 16 if dma else 1
        self.counts[eng] += inc
        self.ops[eng].append((fn, waits, inc))
        tok = (eng, self.counts[eng])
        for b in reads:
            b.readers.append(tok)
        for b in writes:
            b.writer = tok
            b.readers = []
        return tok

    def emit_group(self, eng, fns, reads=(), writes=()):
        need = {}
        for b in reads:
            if b.writer is not None:
                pe, pc = b.writer
                need[pe] = max(need.get(pe, 0), pc)
        for b in writes:
            if b.writer is not None:
                pe, pc = b.writer
                need[pe] = max(need.get(pe, 0), pc)
            for (pe, pc) in b.readers:
                need[pe] = max(need.get(pe, 0), pc)
        waits = []
        for pe in ENGINES:
            pc = need.get(pe, 0)
            if pc > self.marks[eng][pe]:
                waits.append((pe, pc))
                self.marks[eng][pe] = pc
        for i, fn in enumerate(fns):
            w = waits if i == 0 else []
            inc = 1 if i == len(fns) - 1 else 0
            self.ops[eng].append((fn, w, inc))
        self.counts[eng] += 1
        tok = (eng, self.counts[eng])
        for b in reads:
            b.readers.append(tok)
        for b in writes:
            b.writer = tok
            b.readers = []
        return tok

    def build(self, nc, sems):
        with nc.Block() as block:
            def mk(name):
                def run(engine):
                    for (fn, waits, inc) in self.ops[name]:
                        for (pe, pc) in waits:
                            engine.wait_ge(sems[pe], pc)
                        ins = fn(engine)
                        if inc:
                            ins.then_inc(sems[name], inc)
                return run
            block.tensor(mk("tensor"))
            block.vector(mk("vector"))
            block.scalar(mk("scalar"))
            block.gpsimd(mk("gpsimd"))
            block.sync(mk("sync"))


class Region:

    def __init__(self, base, base_esz=4):
        self.base = base
        self.base_esz = base_esz
        self.live = []
        self.floor = {}

    def fence(self):
        for b in self.live:
            toks = list(b.readers)
            if b.writer is not None:
                toks.append(b.writer)
            for (pe, pc) in toks:
                self.floor[pe] = max(self.floor.get(pe, 0), pc)
        self.live = []
        return [(pe, pc) for pe, pc in self.floor.items()]

    def buf(self, name, seed):
        b = Buf(name, seed)
        self.live.append(b)
        return b

    def carve(self, off, n, dt, seed, name="c"):
        esz = 4 if dt in (F32, F32R) else (2 if dt is BF16 else 1)
        b = self.base_esz
        assert off % b == 0 and (n * esz) % b == 0
        ap = self.base[:, off // b: off // b + (n * esz) // b]
        if ap.dtype is not dt:
            ap = ap.bitcast(dt)
        return ap, self.buf(name, seed)


def build_nc():
    nc = bass.Bass("TRN2", target_bir_lowering=False, debug=False,
                   dynamic_dma_scratch_size=8192)
    pr = Prog()
    ctx = ExitStack()

    d_in = {}
    for name, shape in [
        ("xT", [D, COLS]),
        ("wu_p", [CH, 128, 2048]), ("wv_p", [CH, 128, 2048]),
        ("wg1_p", [HCH, 128, 2048]), ("wg2_p", [CH, 128, 512]),
        ("wq_p", [CH, 128, 2048]), ("wk_p", [CH, 128, 2048]),
        ("wvb_p", [CH, 128, 2048]), ("wo_p", [CH, 128, 2048]),
        ("g0_p", [JCH, 128, 2048]), ("g1_p", [JCH, 128, 2048]),
        ("lin_p", [CH, 128, 8192]),
        ("smalls", [128, 120]), ("onescol", [128, 2]),
        ("onesrow", [1, 128]), ("ones82", [8, 2]), ("mask8", [8, 1]),
    ]:
        d_in[name] = nc.dram_tensor(name, shape, F32, kind="ExternalInput").ap()
    out_d = nc.dram_tensor("out", [D, TOK], F32, kind="ExternalOutput").ap()

    xsp_d = nc.dram_tensor("xsp", [128, REGB // 4], F32R, kind="Internal").ap()
    ugd_d = nc.dram_tensor("ugd", [R, TOK], F32R, kind="Internal").ap()
    qd_d = nc.dram_tensor("qd", [D, TOK], F32R, kind="Internal").ap()
    cc_in = [nc.dram_tensor(f"ccin{i}", [128, 16], F32, kind="Internal")
             for i in range(2)]
    cc_out = [nc.dram_tensor(f"ccout{i}", [1024, 16], F32, kind="Internal")
              for i in range(2)]

    rx_t = ctx.enter_context(nc.sbuf_tensor("rx", [128, REGB // 4], F32R))
    ry_t = ctx.enter_context(nc.sbuf_tensor("ry", [128, REGB // 4], F32R))
    c1_t = ctx.enter_context(nc.sbuf_tensor("c1", [128, 69632], FP8))
    c2_t = ctx.enter_context(nc.sbuf_tensor("c2", [128, 4096], F32R))
    RX, RY = Region(rx_t[:]), Region(ry_t[:])
    C1, C2 = Region(c1_t[:], base_esz=1), Region(c2_t[:])

    sm_t = ctx.enter_context(nc.sbuf_tensor("smalls_t", [128, 120], F32))
    oc_t = ctx.enter_context(nc.sbuf_tensor("onescol_t", [128, 2], F32R))
    orow_t = ctx.enter_context(nc.sbuf_tensor("onesrow_t", [1, 128], F32R))
    o82_t = ctx.enter_context(nc.sbuf_tensor("ones82_t", [8, 2], F32R))
    msk_t = ctx.enter_context(nc.sbuf_tensor("mask_t", [8, 1], F32))
    tot_t = ctx.enter_context(nc.sbuf_tensor("tot_t", [128, 16], F32))
    car_t = ctx.enter_context(nc.sbuf_tensor("car_t", [128, 32], F32))
    end_t = ctx.enter_context(nc.sbuf_tensor("end_t", [1, 2], F32))

    ps_t = ctx.enter_context(nc.psum_tensor("ps", [128, 4096], F32))
    PS = ps_t[:]

    sems = {e: ctx.enter_context(nc.semaphore(name=f"sem_{e}")) for e in ENGINES}

    xB = [RX.buf(f"x{c}", ()) for c in range(CH)]
    yB = [RY.buf(f"y{c}", ()) for c in range(CH)]
    smB, ocB, orB, o82B, mskB = Buf("sm"), Buf("oc"), Buf("or"), Buf("o82"), Buf("msk")
    totB, carB = Buf("tot"), Buf("car")
    xspB, ugdB, qdB = Buf("xsp"), Buf("ugd"), Buf("qd")
    ccB = [Buf("cc0"), Buf("cc1")]
    outB = Buf("outd")
    psB = [Buf(f"ps{i}") for i in range(4)]
    ps_off = [0, 1024, 2048, 3072]
    ps_rot = [0]

    def ps_next():
        i = ps_rot[0] % 4
        ps_rot[0] += 1
        return PS[:, ps_off[i]:ps_off[i] + 1024], psB[i]

    rxF, ryF = rx_t[:], ry_t[:]

    def xc(c):
        return rxF[:, c * COLS:(c + 1) * COLS]

    def yc(c):
        return ryF[:, c * COLS:(c + 1) * COLS]

    E = pr.emit

    E("scalar", lambda e: e.dma_start(out=sm_t[:], in_=d_in["smalls"]), writes=[smB], dma=True)
    E("scalar", lambda e: e.dma_start(out=oc_t[:], in_=d_in["onescol"].bitcast(F32R)), writes=[ocB], dma=True)
    E("scalar", lambda e: e.dma_start(out=orow_t[:], in_=d_in["onesrow"].bitcast(F32R)), writes=[orB], dma=True)
    E("scalar", lambda e: e.dma_start(out=o82_t[:], in_=d_in["ones82"].bitcast(F32R)), writes=[o82B], dma=True)
    E("scalar", lambda e: e.dma_start(out=msk_t[:], in_=d_in["mask8"]), writes=[mskB], dma=True)
    for c in range(CH):
        E("scalar" if c % 2 else "sync", lambda e, c=c: e.dma_start(
            out=xc(c), in_=d_in["xT"][c * 128:(c + 1) * 128, :].bitcast(F32R)),
          writes=[xB[c]], dma=True)

    def ln():
        seed = C2.fence()
        rowm = C2.carve(0, COLS, F32R, seed, "rowm")
        rowv = C2.carve(4112, COLS, F32R, seed, "rowv")
        rowr = C2.carve(8224, COLS, F32R, seed, "rowr")
        rm, rv, rr = rowm[0][0:1, :], rowv[0][0:1, :], rowr[0][0:1, :]
        yseed = RY.fence()
        for c in range(CH):
            yB[c] = RY.buf(f"y{c}", yseed)
        splits = [(0, 512), (512, 512), (1024, 4)]
        for c in range(CH):
            E("scalar", lambda e, c=c: e.activation(yc(c), xc(c), AF.Square),
              reads=[xB[c]], writes=[yB[c]])
            fns = []
            for (o, n) in splits:
                fns.append(lambda e, c=c, o=o, n=n: e.matmul(
                    PS[0:1, o:o + n], oc_t[:][:, 0:1], xc(c)[:, o:o + n],
                    start=(c == 0), stop=(c == CH - 1)))
                fns.append(lambda e, c=c, o=o, n=n: e.matmul(
                    PS[0:1, 2048 + o:2048 + o + n], oc_t[:][:, 0:1], yc(c)[:, o:o + n],
                    start=(c == 0), stop=(c == CH - 1)))
            pr.emit_group("tensor", fns, reads=[xB[c], yB[c], ocB],
                          writes=[psB[0], psB[1], psB[2], psB[3]])
        E("vector", lambda e, a=rm: e.tensor_scalar_mul(a, PS[0:1, 0:COLS], 1.0 / D),
          reads=[psB[0], psB[1]], writes=[rowm[1]])
        E("vector", lambda e, a=rv: e.tensor_scalar_mul(a, PS[0:1, 2048:2048 + COLS], 1.0 / D),
          reads=[psB[2], psB[3]], writes=[rowv[1]])
        E("vector", lambda e, a=rr, m=rm: e.tensor_mul(a, m, m),
          reads=[rowm[1]], writes=[rowr[1]])
        E("vector", lambda e, a=rv, m2=rr: e.tensor_sub(a, a, m2),
          reads=[rowr[1], rowv[1]], writes=[rowv[1]])
        E("vector", lambda e, a=rv: e.tensor_scalar_add(a, a, EPS),
          reads=[rowv[1]], writes=[rowv[1]])
        E("scalar", lambda e, a=rv: e.activation(a, a, AF.Sqrt),
          reads=[rowv[1]], writes=[rowv[1]])
        E("vector", lambda e, a=rr, v=rv: e.reciprocal(a, v),
          reads=[rowv[1]], writes=[rowr[1]])
        E("vector", lambda e, a=rm, r=rr: e.scalar_tensor_tensor(
            a, a, -1.0, r, ALU.mult, ALU.mult),
          reads=[rowr[1], rowm[1]], writes=[rowm[1]])
        for (o, n) in splits:
            E("tensor", lambda e, o=o, n=n, r=rr: e.matmul(
                PS[:, o:o + n], orow_t[:], r[0:1, o:o + n], start=True, stop=True),
              reads=[rowr[1], orB], writes=[psB[0] if o < 1024 else psB[1]])
            E("tensor", lambda e, o=o, n=n, r=rm: e.matmul(
                PS[:, 2048 + o:2048 + o + n], orow_t[:], r[0:1, o:o + n], start=True, stop=True),
              reads=[rowm[1], orB], writes=[psB[2] if o < 1024 else psB[3]])
        for c in range(CH):
            E("vector", lambda e, c=c: e.tensor_mul(yc(c), xc(c), PS[:, 0:COLS]),
              reads=[xB[c], psB[0], psB[1]], writes=[yB[c]])
            E("vector", lambda e, c=c: e.tensor_add(yc(c), yc(c), PS[:, 2048:2048 + COLS]),
              reads=[psB[2], psB[3], yB[c]], writes=[yB[c]])

    def spill_x():
        E("scalar", lambda e: e.dma_start(out=xsp_d, in_=rx_t[:]),
          reads=list(xB), writes=[xspB], dma=True)

    def restore_x(seed):
        for c in range(CH):
            xB[c] = RX.buf(f"x{c}", seed)
            E("scalar", lambda e, c=c: e.dma_start(
                out=xc(c), in_=xsp_d[:, c * COLS:(c + 1) * COLS]),
              reads=[xspB], writes=[xB[c]], dma=True)

    def carry_exchange(idx):
        cin, cout = cc_in[idx].ap(), cc_out[idx].ap()
        gma, gmb = RY.carve(4096 * 4, 2048, F32R, RY.fence(), "gm")
        gm = gma[0:8, :]
        E("scalar", lambda e, cin=cin: e.dma_start(out=cin, in_=tot_t[:]),
          reads=[totB], writes=[ccB[idx]], dma=True)
        E("gpsimd", lambda e, i=idx: e.collective_compute(
            "AllGather", ALU.bypass, [list(range(NCORE))],
            [cc_in[i].ap().opt()], [cc_out[i].ap().opt()]),
          reads=[ccB[idx]], writes=[ccB[idx]])
        E("scalar", lambda e, cout=cout, gm=gm: e.dma_start(
            out=gm, in_=cout.rearrange("(r p) c -> r (p c)", p=128).bitcast(F32R)),
          reads=[ccB[idx]], writes=[gmb], dma=True)
        E("vector", lambda e, gm=gm: e.tensor_scalar_mul(gm, gm, msk_t[:][:, 0:1]),
          reads=[mskB, gmb], writes=[gmb])
        gmr = gm.rearrange("r (p c) -> r c p", c=16)
        for m in range(CH):
            E("tensor", lambda e, m=m, g=gmr: e.matmul(
                PS[:, 3072 + 2 * m:3072 + 2 * m + 2], g[:, m, :], o82_t[:],
                start=True, stop=True),
              reads=[gmb, o82B], writes=[psB[3]])
        E("vector", lambda e: e.tensor_copy(car_t[:], PS[:, 3072:3104]),
          reads=[psB[3]], writes=[carB])

    def mm_acc(lhs_ap_fn, lhs_buf, rhs_fn, rhs_buf_fn, nk):
        psap, psb = ps_next()
        fns = []
        for k in range(nk):
            for h in range(2):
                fns.append(lambda e, k=k, h=h, p=psap: e.matmul(
                    p[:, h * 512:(h + 1) * 512], lhs_ap_fn(k), rhs_fn(k, h),
                    start=(k == 0), stop=(k == nk - 1)))
        pr.emit_group("tensor", fns,
                      reads=[lhs_buf] + [rhs_buf_fn(k) for k in range(nk)],
                      writes=[psb])
        return psap, psb

    def y_rhs(k, h):
        return yc(k)[:, RL + h * 512: RL + (h + 1) * 512]

    def y_buf(k):
        return yB[k]

    ph = [0]

    def phase(f):
        if ph[0] < N_PHASES:
            f()
        ph[0] += 1

    phase(ln)

    def p_conv():
        seed = C1.fence()
        ct = [C1.carve(45056, TOK, F32, seed, "cv0"),
              C1.carve(49152, TOK, F32, seed, "cv1")]
        for c in range(CH):
            a, ab = ct[c % 2]
            cw0 = sm_t[:][:, S_CW + 3 * c: S_CW + 3 * c + 1]
            cw1 = sm_t[:][:, S_CW + 3 * c + 1: S_CW + 3 * c + 2]
            cw2 = sm_t[:][:, S_CW + 3 * c + 2: S_CW + 3 * c + 3]
            cbv = sm_t[:][:, S_CB + c: S_CB + c + 1]
            E("vector", lambda e, c=c, a=a, w=cw0, bv=cbv: e.tensor_scalar(
                a, yc(c)[:, 1:1025], w, bv, ALU.mult, ALU.add),
              reads=[yB[c], smB], writes=[ab])
            E("vector", lambda e, c=c, a=a, w=cw1: e.scalar_tensor_tensor(
                a, yc(c)[:, 2:1026], w, a, ALU.mult, ALU.add),
              reads=[yB[c], smB, ab], writes=[ab])
            E("vector", lambda e, c=c, a=a, w=cw2: e.scalar_tensor_tensor(
                a, yc(c)[:, 3:1027], w, a, ALU.mult, ALU.add),
              reads=[yB[c], smB, ab], writes=[ab])
            E("vector", lambda e, c=c, a=a: e.tensor_add(
                xc(c)[:, RL:RH], xc(c)[:, RL:RH], a),
              reads=[ab, xB[c]], writes=[xB[c]])
    phase(p_conv)

    def p_attn_a_proj():
        spill_x()
        seed = RX.fence()
        wsA = [RX.carve(0, 2048, F32R, seed, "wsA0"),
               RX.carve(8192, 2048, F32R, seed, "wsA1")]
        wsB = [RX.carve(16384, 2048, F32R, seed, "wsB0"),
               RX.carve(24576, 2048, F32R, seed, "wsB1")]
        wg2f = [RX.carve(32768, 512, F32R, seed, "wg2f0"),
                RX.carve(34816, 512, F32R, seed, "wg2f1")]
        gtmp = RX.carve(36864, TOK, F32R, seed, "gtmp")
        ugt = [RX.carve(40960, TOK, F32R, seed, "ugt0"),
               RX.carve(45056, TOK, F32R, seed, "ugt1")]
        cseed = C1.fence()
        vgc = [C1.carve(c * 2048, TOK, BF16, cseed, f"vg{c}") for c in range(CH)]
        hc = [C1.carve(32768 + k * 2048, TOK, BF16, cseed, f"h{k}") for k in range(HCH)]
        wg2b = [C1.carve(40960, 512, BF16, cseed, "wg2b0"),
                C1.carve(41984, 512, BF16, cseed, "wg2b1")]
        for m in range(HCH):
            buf = wsA[m % 2]
            E("sync", lambda e, m=m, a=buf[0]: e.dma_start(
                out=a, in_=d_in["wg1_p"][m].bitcast(F32R)), writes=[buf[1]], dma=True)
            psap, psb = mm_acc(
                lambda k, a=buf[0]: a[:, k * 128:(k + 1) * 128], buf[1],
                y_rhs, y_buf, CH)
            E("scalar", lambda e, m=m, p=psap, a=hc[m][0]: e.activation(
                a, p, AF.Relu, bias=sm_t[:][:, S_BG1 + m:S_BG1 + m + 1]),
              reads=[psb, smB], writes=[hc[m][1]])
        for m in range(CH):
            bufu, bufv = wsA[m % 2], wsB[m % 2]
            bufg, bufgb = wg2f[m % 2], wg2b[m % 2]
            E("sync", lambda e, m=m, a=bufg[0]: e.dma_start(
                out=a, in_=d_in["wg2_p"][m].bitcast(F32R)), writes=[bufg[1]], dma=True)
            E("sync", lambda e, m=m, a=bufu[0]: e.dma_start(
                out=a, in_=d_in["wu_p"][m].bitcast(F32R)), writes=[bufu[1]], dma=True)
            E("sync", lambda e, m=m, a=bufv[0]: e.dma_start(
                out=a, in_=d_in["wv_p"][m].bitcast(F32R)), writes=[bufv[1]], dma=True)
            E("scalar", lambda e, a=bufgb[0], f=bufg[0]: e.copy(a, f),
              reads=[bufg[1]], writes=[bufgb[1]])
            psg, psgb = mm_acc(
                lambda k, a=bufgb[0]: a[:, k * 128:(k + 1) * 128], bufgb[1],
                lambda k, h: hc[k][0][:, h * 512:(h + 1) * 512],
                lambda k: hc[k][1], HCH)
            E("scalar", lambda e, a=gtmp[0], p=psg, m=m: e.activation(
                a, p, AF.Sigmoid, bias=sm_t[:][:, S_BG2 + m:S_BG2 + m + 1]),
              reads=[psgb, smB], writes=[gtmp[1]])
            psu, psub = mm_acc(
                lambda k, a=bufu[0]: a[:, k * 128:(k + 1) * 128], bufu[1],
                y_rhs, y_buf, CH)
            ug = ugt[m % 2]
            E("vector", lambda e, a=ug[0], p=psu, m=m, g=gtmp[0]: e.scalar_tensor_tensor(
                a, p, sm_t[:][:, S_BU + m:S_BU + m + 1], g, ALU.add, ALU.mult),
              reads=[psub, smB, gtmp[1]], writes=[ug[1]])
            E("vector", lambda e, a=ug[0], m=m: e.reduce_sum(
                tot_t[:][:, m:m + 1], a, axis=AX),
              reads=[ug[1]], writes=[totB])
            E("scalar", lambda e, a=ug[0], m=m: e.dma_start(
                out=ugd_d[m * 128:(m + 1) * 128, :], in_=a),
              reads=[ug[1]], writes=[ugdB], dma=True)
            psv, psvb = mm_acc(
                lambda k, a=bufv[0]: a[:, k * 128:(k + 1) * 128], bufv[1],
                y_rhs, y_buf, CH)
            E("vector", lambda e, m=m, p=psv, g=gtmp[0], a=vgc[m][0]: e.scalar_tensor_tensor(
                a, p, sm_t[:][:, S_BV + m:S_BV + m + 1], g, ALU.add, ALU.mult),
              reads=[psvb, smB, gtmp[1]], writes=[vgc[m][1]])
        return vgc
    vgc_holder = []
    phase(lambda: vgc_holder.append(p_attn_a_proj()))

    gc_holder = []

    def p_attn_a_scan():
        vgc = vgc_holder[0]
        carry_exchange(0)
        cseed = C1.fence()
        ugr = [C1.carve(45056, TOK, F32R, cseed, "ugr0"),
               C1.carve(49152, TOK, F32R, cseed, "ugr1")]
        ktmp = C1.carve(53248, TOK, F32R, cseed, "ktmp")
        yseed = RY.fence()
        gc = [RY.carve(c * 4096, TOK, F32R, yseed, f"g{c}") for c in range(CH)]
        gc_holder.append(gc)
        for m in range(CH):
            u = ugr[m % 2]
            E("scalar", lambda e, a=u[0], m=m: e.dma_start(
                out=a, in_=ugd_d[m * 128:(m + 1) * 128, :]),
              reads=[ugdB], writes=[u[1]], dma=True)
            E("vector", lambda e, a=ktmp[0], u=u[0], m=m: e.tensor_tensor_scan(
                a, u, u, car_t[:][:, 2 * m:2 * m + 1], ALU.add, ALU.bypass),
              reads=[u[1], carB], writes=[ktmp[1]])
            E("vector", lambda e, m=m, a=ktmp[0], g=gc[m][0], v=vgc[m][0]: e.tensor_mul(
                g, a, v),
              reads=[ktmp[1], vgc[m][1]], writes=[gc[m][1]])
    phase(p_attn_a_scan)

    def p_attn_a_uv():
        kuv = os.environ.get("KUV", "all")
        gc = gc_holder[0]
        seed = C2.fence()
        ws = [C2.carve(0, 2048, F32R, seed, "uvw0"),
              C2.carve(8192, 2048, F32R, seed, "uvw1")]
        if kuv != "nores":
            restore_x(RX.fence())
        for mo in range(CH):
            if kuv == "restonly":
                continue
            buf = ws[mo % 2]
            E("sync", lambda e, mo=mo, a=buf[0]: e.dma_start(
                out=a, in_=d_in["wu_p"][mo].bitcast(F32R)), writes=[buf[1]], dma=True)
            psap, psb = mm_acc(
                lambda k, a=buf[0]: a[:, k * 128:(k + 1) * 128], buf[1],
                lambda k, h: gc[k][0][:, h * 512:(h + 1) * 512],
                lambda k: gc[k][1], CH)
            if kuv == "all":
                E("vector", lambda e, mo=mo, p=psap: e.tensor_add(
                    xc(mo)[:, RL:RH], xc(mo)[:, RL:RH], p),
                  reads=[psb, xB[mo]], writes=[xB[mo]])
    phase(p_attn_a_uv)

    def p_ffn(final):
        kffn = os.environ.get("KFFN", "all") if final else "all"
        if kffn in ("xpre", "xln", "xspill", "xdump"):
            if kffn == "xln":
                ln()
            if kffn == "xspill":
                spill_x()
                restore_x(RX.fence())
            if kffn == "xdump":
                spill_x()
                for mo in range(CH):
                    E("sync", lambda e, mo=mo: e.dma_start(
                        out=out_d[mo * 128:(mo + 1) * 128, :].bitcast(F32R),
                        in_=xsp_d[:, mo * COLS + RL: mo * COLS + RH]),
                      reads=[xspB], writes=[outB], dma=True)
                E("vector", lambda e: e.memset(end_t[0:1, 0:1], 0.0), reads=[outB])
                return
            yseed = RY.fence()
            otl = [RY.carve(0, TOK, F32R, yseed, "ot0"),
                   RY.carve(4096, TOK, F32R, yseed, "ot1")]
            for mo in range(CH):
                o = otl[mo % 2]
                E("vector", lambda e, mo=mo, a=o[0]: e.tensor_scalar_mul(
                    a, xc(mo)[:, RL:RH], 1.0),
                  reads=[xB[mo]], writes=[o[1]])
                E("sync", lambda e, mo=mo, a=o[0]: e.dma_start(
                    out=out_d[mo * 128:(mo + 1) * 128, :].bitcast(F32R), in_=a),
                  reads=[o[1]], writes=[outB], dma=True)
            return
        ln()
        spill_x()
        xseed = RX.fence()
        g0 = [RX.carve(0, 2048, F32R, xseed, "g0a"),
              RX.carve(8192, 2048, F32R, xseed, "g0b")]
        g1 = [RX.carve(16384, 2048, F32R, xseed, "g1a"),
              RX.carve(24576, 2048, F32R, xseed, "g1b")]
        gel = RX.carve(32768, TOK, F32R, xseed, "gel")
        cseed = C1.fence()
        a8 = [C1.carve(j * 1024, TOK, FP8, cseed, f"a8{j}") for j in range(JCH)]
        for j in range(JCH if kffn != "xonly" else 0):
            b0, b1 = g0[j % 2], g1[j % 2]
            E("sync", lambda e, j=j, a=b0[0]: e.dma_start(
                out=a, in_=d_in["g0_p"][j].bitcast(F32R)), writes=[b0[1]], dma=True)
            E("sync", lambda e, j=j, a=b1[0]: e.dma_start(
                out=a, in_=d_in["g1_p"][j].bitcast(F32R)), writes=[b1[1]], dma=True)
            psg, psgb = mm_acc(
                lambda k, a=b0[0]: a[:, k * 128:(k + 1) * 128], b0[1],
                y_rhs, y_buf, CH)
            E("scalar", lambda e, a=gel[0], p=psg: e.activation(a, p, AF.Gelu_apprx_tanh),
              reads=[psgb], writes=[gel[1]])
            psf, psfb = mm_acc(
                lambda k, a=b1[0]: a[:, k * 128:(k + 1) * 128], b1[1],
                y_rhs, y_buf, CH)
            E("vector", lambda e, j=j, a=gel[0], p=psf, o=a8[j][0]: e.tensor_mul(o, a, p),
              reads=[gel[1], psfb], writes=[a8[j][1]])
        c2seed = C2.fence()
        vfq = C2.carve(0, 2048, F32R, c2seed, "vfq")
        v8 = [C1.carve(65536, 2048, FP8, C1.fence(), "v80"),
              C1.carve(67584, 2048, FP8, (), "v81")]
        otl = None
        if final:
            yseed = RY.fence()
            otl = [RY.carve(0, TOK, F32R, yseed, "ot0"),
                   RY.carve(4096, TOK, F32R, yseed, "ot1")]
        restore_x(RX.fence())
        if kffn == "xonly":
            for mo in range(CH):
                o = otl[mo % 2]
                E("vector", lambda e, mo=mo, a=o[0]: e.tensor_scalar_mul(
                    a, xc(mo)[:, RL:RH], 1.0),
                  reads=[xB[mo]], writes=[o[1]])
                E("sync", lambda e, mo=mo, a=o[0]: e.dma_start(
                    out=out_d[mo * 128:(mo + 1) * 128, :].bitcast(F32R), in_=a),
                  reads=[o[1]], writes=[outB], dma=True)
            return
        for mo in range(CH):
            psap, psb = ps_next()
            for qq in range(4):
                q8 = v8[qq % 2]
                E("sync", lambda e, mo=mo, qq=qq, a=vfq[0]: e.dma_start(
                    out=a, in_=d_in["lin_p"][mo][:, qq * 2048:(qq + 1) * 2048].bitcast(F32R)),
                  reads=[], writes=[vfq[1]], dma=True)
                E("scalar", lambda e, a=q8[0], f=vfq[0]: e.copy(a, f),
                  reads=[vfq[1]], writes=[q8[1]])
                fns = []
                abufs = []
                for jj2 in range(8):
                    j0 = qq * 16 + jj2 * 2
                    abufs.append(a8[j0][1])
                    abufs.append(a8[j0 + 1][1])
                    lhs = q8[0][:, jj2 * 256:(jj2 + 1) * 256].rearrange(
                        "p (j m) -> p j m", j=2)
                    rhs_pair = c1_t[:][:, j0 * 1024:(j0 + 2) * 1024].rearrange(
                        "p (j t) -> p j t", j=2)
                    for h in range(2):
                        fns.append(lambda e, j0=j0, h=h, l=lhs, r=rhs_pair, p=psap: e.matmul(
                            p[:, h * 512:(h + 1) * 512], l, r[:, :, h * 512:(h + 1) * 512],
                            start=(j0 == 0), stop=(j0 == JCH - 2),
                            perf_mode=mybir.MatmulPerfMode.DoubleRow))
                pr.emit_group("tensor", fns, reads=[q8[1]] + abufs, writes=[psb])
            if final:
                o = otl[mo % 2]
                E("vector", lambda e, mo=mo, p=psap, a=o[0]: e.tensor_add(
                    a, xc(mo)[:, RL:RH], p),
                  reads=[psb, xB[mo]], writes=[o[1]])
                E("sync", lambda e, mo=mo, a=o[0]: e.dma_start(
                    out=out_d[mo * 128:(mo + 1) * 128, :].bitcast(F32R), in_=a),
                  reads=[o[1]], writes=[outB], dma=True)
            else:
                E("vector", lambda e, mo=mo, p=psap: e.tensor_add(
                    xc(mo)[:, RL:RH], xc(mo)[:, RL:RH], p),
                  reads=[psb, xB[mo]], writes=[xB[mo]])
    phase(lambda: p_ffn(False))

    def p_attn_b():
        katb = os.environ.get("KATB", "all")
        ln()
        spill_x()
        xseed = RX.fence()
        wq = [RX.carve(0, 2048, F32R, xseed, "wq0"),
              RX.carve(8192, 2048, F32R, xseed, "wq1")]
        wk = [RX.carve(16384, 2048, F32R, xseed, "wk0"),
              RX.carve(24576, 2048, F32R, xseed, "wk1")]
        wv = [RX.carve(32768, 2048, F32R, xseed, "wv0"),
              RX.carve(40960, 2048, F32R, xseed, "wv1")]
        ktm = RX.carve(49152, TOK, F32R, xseed, "ktm")
        qtm = [RX.carve(53248, TOK, F32R, xseed, "qt0"),
               RX.carve(57344, TOK, F32R, xseed, "qt1")]
        cseed = C1.fence()
        kvc = [C1.carve(c * 4096, TOK, F32, cseed, f"kv{c}") for c in range(CH)]
        if katb == "ln":
            return
        nM = int(katb[1:]) if katb.startswith("m") else CH
        for m in range(nM):
            bq, bk, bv = wq[m % 2], wk[m % 2], wv[m % 2]
            E("sync", lambda e, m=m, a=bq[0]: e.dma_start(
                out=a, in_=d_in["wq_p"][m].bitcast(F32R)), writes=[bq[1]], dma=True)
            E("sync", lambda e, m=m, a=bk[0]: e.dma_start(
                out=a, in_=d_in["wk_p"][m].bitcast(F32R)), writes=[bk[1]], dma=True)
            E("sync", lambda e, m=m, a=bv[0]: e.dma_start(
                out=a, in_=d_in["wvb_p"][m].bitcast(F32R)), writes=[bv[1]], dma=True)
            if katb != "noq":
                psq, psqb = mm_acc(
                    lambda k, a=bq[0]: a[:, k * 128:(k + 1) * 128], bq[1],
                    y_rhs, y_buf, CH)
                if katb != "qne":
                    qb = qtm[m % 2]
                    E("scalar", lambda e, a=qb[0], p=psq: e.activation(a, p, AF.Copy, scale=DENOM),
                      reads=[psqb], writes=[qb[1]])
                    if katb != "qns":
                        E("sync", lambda e, a=qb[0], m=m: e.dma_start(
                            out=qd_d[m * 128:(m + 1) * 128, :], in_=a),
                          reads=[qb[1]], writes=[qdB], dma=True)
            psk, pskb = mm_acc(
                lambda k, a=bk[0]: a[:, k * 128:(k + 1) * 128], bk[1],
                y_rhs, y_buf, CH)
            E("vector", lambda e, a=ktm[0], p=psk: e.tensor_copy(a, p),
              reads=[pskb], writes=[ktm[1]])
            psv, psvb = mm_acc(
                lambda k, a=bv[0]: a[:, k * 128:(k + 1) * 128], bv[1],
                y_rhs, y_buf, CH)
            E("vector", lambda e, m=m, a=ktm[0], p=psv, o=kvc[m][0]: e.tensor_mul(o, a, p),
              reads=[ktm[1], psvb], writes=[kvc[m][1]])
            E("vector", lambda e, m=m, o=kvc[m][0]: e.reduce_sum(
                tot_t[:][:, m:m + 1], o, axis=AX),
              reads=[kvc[m][1]], writes=[totB])
        if katb == "proj":
            return
        carry_exchange(1)
        if katb == "cc":
            return
        yseed = RY.fence()
        vh = [RY.carve(c * 4096, TOK, F32R, yseed, f"vh{c}") for c in range(CH)]
        for m in range(CH):
            qb = qtm[m % 2]
            E("scalar", lambda e, a=qb[0], m=m: e.dma_start(
                out=a, in_=qd_d[m * 128:(m + 1) * 128, :]),
              reads=[qdB], writes=[qb[1]], dma=True)
            E("vector", lambda e, a=ktm[0], m=m, u=kvc[m][0]: e.tensor_tensor_scan(
                a, u, u, car_t[:][:, 2 * m:2 * m + 1], ALU.add, ALU.bypass),
              reads=[kvc[m][1], carB], writes=[ktm[1]])
            E("vector", lambda e, m=m, q=qb[0], a=ktm[0], o=vh[m][0]: e.tensor_mul(o, q, a),
              reads=[qb[1], ktm[1]], writes=[vh[m][1]])
        if katb == "scan":
            return
        c2seed = C2.fence()
        wo = [C2.carve(0, 2048, F32R, c2seed, "wo0"),
              C2.carve(8192, 2048, F32R, c2seed, "wo1")]
        restore_x(RX.fence())
        for mo in range(CH):
            buf = wo[mo % 2]
            E("sync", lambda e, mo=mo, a=buf[0]: e.dma_start(
                out=a, in_=d_in["wo_p"][mo].bitcast(F32R)), writes=[buf[1]], dma=True)
            psap, psb = mm_acc(
                lambda k, a=buf[0]: a[:, k * 128:(k + 1) * 128], buf[1],
                lambda k, h: vh[k][0][:, h * 512:(h + 1) * 512],
                lambda k: vh[k][1], CH)
            E("vector", lambda e, mo=mo, p=psap: e.tensor_add(
                xc(mo)[:, RL:RH], xc(mo)[:, RL:RH], p),
              reads=[psb, xB[mo]], writes=[xB[mo]])
    phase(p_attn_b)

    phase(lambda: p_ffn(True))

    if N_PHASES < 8:
        yseed = RY.fence()
        ot = [RY.carve(0, TOK, F32R, yseed, "dbg0"),
              RY.carve(4096, TOK, F32R, yseed, "dbg1")]
        for mo in range(CH):
            o = ot[mo % 2]
            E("vector", lambda e, mo=mo, a=o[0]: e.tensor_scalar_mul(
                a, xc(mo)[:, RL:RH], 1.0),
              reads=[xB[mo]], writes=[o[1]])
            E("sync", lambda e, mo=mo, a=o[0]: e.dma_start(
                out=out_d[mo * 128:(mo + 1) * 128, :].bitcast(F32R), in_=a),
              reads=[o[1]], writes=[outB], dma=True)

    fin = [(e, pr.counts[e]) for e in ENGINES if e != "vector" and pr.counts[e] > 0]
    endB = Buf("end")
    E("vector", lambda e: e.memset(end_t[:], 0.0), writes=[endB])
    pr.ops["vector"].append((lambda e: e.memset(end_t[:], 1.0), fin, 1))
    pr.counts["vector"] += 1

    with nc.allow_low_precision(reason="deliberate f32r/bf16/fp8 stores; f32 PSUM accumulation throughout"):
        pr.build(nc, sems)
    ctx.close()
    return nc


def _pack2d(A):
    K, M = A.shape
    return np.ascontiguousarray(
        A.reshape(K // 128, 128, M // 128, 128).transpose(2, 1, 0, 3)
        .reshape(M // 128, 128, K))


_NC_CACHE = [None]


def kernel(L0_attn_a_x, L0_attn_a_Wu, L0_attn_a_bu, L0_attn_a_Wv, L0_attn_a_bv,
           L0_attn_a_Wg1, L0_attn_a_bg1, L0_attn_a_Wg2, L0_attn_a_bg2,
           L0_attn_a_conv_w, L0_attn_a_conv_b, L0_attn_b_Wq, L0_attn_b_Wk,
           L0_attn_b_Wv, L0_attn_b_Wo, L0_ffn_w_gating, L0_ffn_w_linear):
    f = np.float32
    x = np.asarray(L0_attn_a_x, f)
    Wu = np.asarray(L0_attn_a_Wu, f); Wv = np.asarray(L0_attn_a_Wv, f)
    Wg1 = np.asarray(L0_attn_a_Wg1, f); Wg2 = np.asarray(L0_attn_a_Wg2, f)
    Wq = np.asarray(L0_attn_b_Wq, f); Wk = np.asarray(L0_attn_b_Wk, f)
    Wvb = np.asarray(L0_attn_b_Wv, f); Wo = np.asarray(L0_attn_b_Wo, f)
    wg = np.asarray(L0_ffn_w_gating, f); wl = np.asarray(L0_ffn_w_linear, f)

    packs = {
        "wu_p": _pack2d(np.ascontiguousarray(Wu.T)),
        "wv_p": _pack2d(np.ascontiguousarray(Wv.T)),
        "wg1_p": _pack2d(np.ascontiguousarray(Wg1.T)),
        "wg2_p": _pack2d(np.ascontiguousarray(Wg2.T)),
        "wq_p": _pack2d(np.ascontiguousarray(Wq.T)),
        "wk_p": _pack2d(np.ascontiguousarray(Wk.T)),
        "wvb_p": _pack2d(np.ascontiguousarray(Wvb.T)),
        "wo_p": _pack2d(np.ascontiguousarray(Wo.T)),
        "g0_p": _pack2d(wg[0]),
        "g1_p": _pack2d(wg[1]),
        "lin_p": _pack2d(wl),
    }
    smalls = np.zeros((128, 120), f)
    smalls[:, S_BU:S_BU + 16] = np.asarray(L0_attn_a_bu, f).reshape(16, 128).T
    smalls[:, S_BV:S_BV + 16] = np.asarray(L0_attn_a_bv, f).reshape(16, 128).T
    smalls[:, S_CB:S_CB + 16] = np.asarray(L0_attn_a_conv_b, f).reshape(16, 128).T
    smalls[:, S_BG2:S_BG2 + 16] = np.asarray(L0_attn_a_bg2, f).reshape(16, 128).T
    smalls[:, S_BG1:S_BG1 + 4] = np.asarray(L0_attn_a_bg1, f).reshape(4, 128).T
    cw = np.asarray(L0_attn_a_conv_w, f).reshape(2048, 3)
    smalls[:, S_CW:S_CW + 48] = cw.reshape(16, 128, 3).transpose(1, 0, 2).reshape(128, 48)

    onescol = np.ones((128, 2), f)
    onesrow = np.ones((1, 128), f)
    ones82 = np.ones((8, 2), f)

    xpad = np.zeros((B, L + 2, D), f)
    xpad[:, 1:L + 1] = x

    in_maps = []
    for c in range(NCORE):
        s, q = c // 4, c % 4
        xTc = np.zeros((COLS, D), f)
        xTc[1:TOK + 3] = xpad[s, q * TOK: q * TOK + TOK + 2, :]
        xT = np.ascontiguousarray(xTc.T)
        mask = np.zeros((8, 1), f)
        for r in range(NCORE):
            if r // 4 == s and r < c:
                mask[r, 0] = 1.0
        m = dict(packs)
        m.update({"xT": xT, "smalls": smalls, "onescol": onescol,
                  "onesrow": onesrow, "ones82": ones82, "mask8": mask})
        in_maps.append(m)

    if _NC_CACHE[0] is None:
        _NC_CACHE[0] = build_nc()
    nc = _NC_CACHE[0]
    res = run_bass_kernel_spmd(nc, in_maps, core_ids=list(range(NCORE)))

    y = np.empty((B, L, D), f)
    for c in range(NCORE):
        s, q = c // 4, c % 4
        y[s, q * TOK:(q + 1) * TOK, :] = res.results[c]["out"].T
    return y


# version 04
# speedup vs baseline: 2.9506x; 1.0274x over previous
import os
import numpy as np
from contextlib import ExitStack

import concourse.bass as bass
import concourse.mybir as mybir
from concourse.bass_utils import run_bass_kernel_spmd

F32 = mybir.dt.float32
F32R = mybir.dt.float32r
BF16 = mybir.dt.bfloat16
FP8 = mybir.dt.float8e4
U8 = mybir.dt.uint8
AF = mybir.ActivationFunctionType
ALU = mybir.AluOpType
AX = mybir.AxisListType.X

B, L, D, R, DH, HID = 2, 4096, 2048, 2048, 512, 8192
NCORE = 8
TOK = 1024
CH = 16
HCH = 4
JCH = 64
COLS = 1028
CB = COLS * 4
REGB = CH * CB
RL, RH = 2, 1026
DENOM = float(np.float64(np.sqrt(np.float32(D))) / L)
EPS = 1e-5

S_BU, S_BV, S_CB, S_BG2, S_BG1, S_CW = 0, 16, 32, 48, 64, 68

ENGINES = ("tensor", "vector", "scalar", "gpsimd", "sync")
NCHAN = 16
SEMS = ENGINES + tuple(f"ch{i}" for i in range(NCHAN))

N_PHASES = int(os.environ.get("KPHASES", "99"))
PARANOID = os.environ.get("KPAR", "0") == "1"
DMASER = os.environ.get("KDMASER", "1") == "1"


class Buf:
    __slots__ = ("name", "writer", "readers")

    def __init__(self, name="b", seed=()):
        self.name = name
        self.writer = None
        self.readers = list(seed)


class Prog:
    def __init__(self):
        self.counts = {e: 0 for e in SEMS}
        self.marks = {e: {p: 0 for p in SEMS} for e in ENGINES}
        self.ops = {e: [] for e in ENGINES}

    def emit(self, eng, fn, reads=(), writes=(), dma=False, chan=None):
        need = {}
        for b in reads:
            if b.writer is not None:
                pe, pc = b.writer
                need[pe] = max(need.get(pe, 0), pc)
        for b in writes:
            if b.writer is not None:
                pe, pc = b.writer
                need[pe] = max(need.get(pe, 0), pc)
            for (pe, pc) in b.readers:
                need[pe] = max(need.get(pe, 0), pc)
        if PARANOID:
            for pe in ENGINES:
                if pe != eng:
                    need[pe] = max(need.get(pe, 0), self.counts[pe])
        if DMASER and dma and chan is None:
            need[eng] = max(need.get(eng, 0), self.counts[eng])
        waits = []
        for pe in SEMS:
            pc = need.get(pe, 0)
            if pc > self.marks[eng][pe]:
                waits.append((pe, pc))
                self.marks[eng][pe] = pc
        inc = 16 if dma else 1
        sname = eng if chan is None else f"ch{chan}"
        self.counts[sname] += inc
        self.ops[eng].append((fn, waits, inc, sname))
        tok = (sname, self.counts[sname])
        for b in reads:
            b.readers.append(tok)
        for b in writes:
            b.writer = tok
            b.readers = []
        return tok

    def emit_group(self, eng, fns, reads=(), writes=()):
        need = {}
        for b in reads:
            if b.writer is not None:
                pe, pc = b.writer
                need[pe] = max(need.get(pe, 0), pc)
        for b in writes:
            if b.writer is not None:
                pe, pc = b.writer
                need[pe] = max(need.get(pe, 0), pc)
            for (pe, pc) in b.readers:
                need[pe] = max(need.get(pe, 0), pc)
        waits = []
        for pe in SEMS:
            pc = need.get(pe, 0)
            if pc > self.marks[eng][pe]:
                waits.append((pe, pc))
                self.marks[eng][pe] = pc
        for i, fn in enumerate(fns):
            w = waits if i == 0 else []
            inc = 1 if i == len(fns) - 1 else 0
            self.ops[eng].append((fn, w, inc, eng))
        self.counts[eng] += 1
        tok = (eng, self.counts[eng])
        for b in reads:
            b.readers.append(tok)
        for b in writes:
            b.writer = tok
            b.readers = []
        return tok

    def build(self, nc, sems):
        with nc.Block() as block:
            def mk(name):
                def run(engine):
                    for (fn, waits, inc, sname) in self.ops[name]:
                        for (pe, pc) in waits:
                            engine.wait_ge(sems[pe], pc)
                        ins = fn(engine)
                        if inc:
                            ins.then_inc(sems[sname], inc)
                return run
            block.tensor(mk("tensor"))
            block.vector(mk("vector"))
            block.scalar(mk("scalar"))
            block.gpsimd(mk("gpsimd"))
            block.sync(mk("sync"))


class Region:

    def __init__(self, base, base_esz=4):
        self.base = base
        self.base_esz = base_esz
        self.live = []
        self.floor = {}

    def fence(self):
        for b in self.live:
            toks = list(b.readers)
            if b.writer is not None:
                toks.append(b.writer)
            for (pe, pc) in toks:
                self.floor[pe] = max(self.floor.get(pe, 0), pc)
        self.live = []
        return [(pe, pc) for pe, pc in self.floor.items()]

    def buf(self, name, seed):
        b = Buf(name, seed)
        self.live.append(b)
        return b

    def carve(self, off, n, dt, seed, name="c"):
        esz = 4 if dt in (F32, F32R) else (2 if dt is BF16 else 1)
        b = self.base_esz
        assert off % b == 0 and (n * esz) % b == 0
        ap = self.base[:, off // b: off // b + (n * esz) // b]
        if ap.dtype is not dt:
            ap = ap.bitcast(dt)
        return ap, self.buf(name, seed)


def build_nc():
    nc = bass.Bass("TRN2", target_bir_lowering=False, debug=False,
                   dynamic_dma_scratch_size=8192)
    pr = Prog()
    ctx = ExitStack()

    d_in = {}
    for name, shape in [
        ("xT", [D, COLS]),
        ("wu_p", [CH, 128, 2048]), ("wv_p", [CH, 128, 2048]),
        ("wg1_p", [HCH, 128, 2048]), ("wg2_p", [CH, 128, 512]),
        ("wq_p", [CH, 128, 2048]), ("wk_p", [CH, 128, 2048]),
        ("wvb_p", [CH, 128, 2048]), ("wo_p", [CH, 128, 2048]),
        ("g0_p", [JCH, 128, 2048]), ("g1_p", [JCH, 128, 2048]),
        ("lin_p", [CH, 128, 8192]),
        ("smalls", [128, 120]), ("onescol", [128, 2]),
        ("onesrow", [1, 128]), ("ones82", [8, 2]), ("mask8", [8, 1]),
    ]:
        d_in[name] = nc.dram_tensor(name, shape, F32, kind="ExternalInput").ap()
    out_d = nc.dram_tensor("out", [D, TOK], F32, kind="ExternalOutput").ap()

    xsp_d = nc.dram_tensor("xsp", [128, REGB // 4], F32R, kind="Internal").ap()
    ugd_d = nc.dram_tensor("ugd", [R, TOK], F32R, kind="Internal").ap()
    qd_d = nc.dram_tensor("qd", [D, TOK], F32R, kind="Internal").ap()
    cc_in = [nc.dram_tensor(f"ccin{i}", [128, 16], F32, kind="Internal")
             for i in range(2)]
    cc_out = [nc.dram_tensor(f"ccout{i}", [1024, 16], F32, kind="Internal")
              for i in range(2)]

    rx_t = ctx.enter_context(nc.sbuf_tensor("rx", [128, REGB // 4], F32R))
    ry_t = ctx.enter_context(nc.sbuf_tensor("ry", [128, REGB // 4], F32R))
    c1_t = ctx.enter_context(nc.sbuf_tensor("c1", [128, 69632], FP8))
    c2_t = ctx.enter_context(nc.sbuf_tensor("c2", [128, 4096], F32R))
    RX, RY = Region(rx_t[:]), Region(ry_t[:])
    C1, C2 = Region(c1_t[:], base_esz=1), Region(c2_t[:])

    sm_t = ctx.enter_context(nc.sbuf_tensor("smalls_t", [128, 120], F32))
    oc_t = ctx.enter_context(nc.sbuf_tensor("onescol_t", [128, 2], F32R))
    orow_t = ctx.enter_context(nc.sbuf_tensor("onesrow_t", [1, 128], F32R))
    o82_t = ctx.enter_context(nc.sbuf_tensor("ones82_t", [8, 2], F32R))
    msk_t = ctx.enter_context(nc.sbuf_tensor("mask_t", [8, 1], F32))
    tot_t = ctx.enter_context(nc.sbuf_tensor("tot_t", [128, 16], F32))
    car_t = ctx.enter_context(nc.sbuf_tensor("car_t", [128, 32], F32))
    end_t = ctx.enter_context(nc.sbuf_tensor("end_t", [1, 2], F32))

    ps_t = ctx.enter_context(nc.psum_tensor("ps", [128, 4096], F32))
    PS = ps_t[:]

    sems = {e: ctx.enter_context(nc.semaphore(name=f"sem_{e}")) for e in SEMS}

    xB = [RX.buf(f"x{c}", ()) for c in range(CH)]
    yB = [RY.buf(f"y{c}", ()) for c in range(CH)]
    smB, ocB, orB, o82B, mskB = Buf("sm"), Buf("oc"), Buf("or"), Buf("o82"), Buf("msk")
    totB, carB = Buf("tot"), Buf("car")
    xspB, ugdB, qdB = Buf("xsp"), Buf("ugd"), Buf("qd")
    ccB = [Buf("cc0"), Buf("cc1")]
    outB = Buf("outd")
    psB = [Buf(f"ps{i}") for i in range(4)]
    ps_off = [0, 1024, 2048, 3072]
    ps_rot = [0]

    def ps_next():
        i = ps_rot[0] % 4
        ps_rot[0] += 1
        return PS[:, ps_off[i]:ps_off[i] + 1024], psB[i]

    rxF, ryF = rx_t[:], ry_t[:]

    def xc(c):
        return rxF[:, c * COLS:(c + 1) * COLS]

    def yc(c):
        return ryF[:, c * COLS:(c + 1) * COLS]

    E = pr.emit

    E("scalar", lambda e: e.dma_start(out=sm_t[:], in_=d_in["smalls"]), writes=[smB], dma=True)
    E("scalar", lambda e: e.dma_start(out=oc_t[:], in_=d_in["onescol"].bitcast(F32R)), writes=[ocB], dma=True)
    E("scalar", lambda e: e.dma_start(out=orow_t[:], in_=d_in["onesrow"].bitcast(F32R)), writes=[orB], dma=True)
    E("scalar", lambda e: e.dma_start(out=o82_t[:], in_=d_in["ones82"].bitcast(F32R)), writes=[o82B], dma=True)
    E("scalar", lambda e: e.dma_start(out=msk_t[:], in_=d_in["mask8"]), writes=[mskB], dma=True)
    E("sync", lambda e: e.dma_start(
        out=rxF[:, 0:CH * COLS].rearrange("p (c t) -> p c t", c=CH),
        in_=d_in["xT"].rearrange("(c p) t -> p c t", p=128).bitcast(F32R)),
      writes=list(xB), dma=True, chan=10)

    def ln():
        seed = C2.fence()
        rowm = C2.carve(0, COLS, F32R, seed, "rowm")
        rowv = C2.carve(4112, COLS, F32R, seed, "rowv")
        rowr = C2.carve(8224, COLS, F32R, seed, "rowr")
        rm, rv, rr = rowm[0][0:1, :], rowv[0][0:1, :], rowr[0][0:1, :]
        yseed = RY.fence()
        for c in range(CH):
            yB[c] = RY.buf(f"y{c}", yseed)
        splits = [(0, 512), (512, 512), (1024, 4)]
        for c in range(CH):
            E("scalar", lambda e, c=c: e.activation(yc(c), xc(c), AF.Square),
              reads=[xB[c]], writes=[yB[c]])
            fns = []
            for (o, n) in splits:
                fns.append(lambda e, c=c, o=o, n=n: e.matmul(
                    PS[0:1, o:o + n], oc_t[:][:, 0:1], xc(c)[:, o:o + n],
                    start=(c == 0), stop=(c == CH - 1)))
                fns.append(lambda e, c=c, o=o, n=n: e.matmul(
                    PS[0:1, 2048 + o:2048 + o + n], oc_t[:][:, 0:1], yc(c)[:, o:o + n],
                    start=(c == 0), stop=(c == CH - 1)))
            pr.emit_group("tensor", fns, reads=[xB[c], yB[c], ocB],
                          writes=[psB[0], psB[1], psB[2], psB[3]])
        E("vector", lambda e, a=rm: e.tensor_scalar_mul(a, PS[0:1, 0:COLS], 1.0 / D),
          reads=[psB[0], psB[1]], writes=[rowm[1]])
        E("vector", lambda e, a=rv: e.tensor_scalar_mul(a, PS[0:1, 2048:2048 + COLS], 1.0 / D),
          reads=[psB[2], psB[3]], writes=[rowv[1]])
        E("vector", lambda e, a=rr, m=rm: e.tensor_mul(a, m, m),
          reads=[rowm[1]], writes=[rowr[1]])
        E("vector", lambda e, a=rv, m2=rr: e.tensor_sub(a, a, m2),
          reads=[rowr[1], rowv[1]], writes=[rowv[1]])
        E("vector", lambda e, a=rv: e.tensor_scalar_add(a, a, EPS),
          reads=[rowv[1]], writes=[rowv[1]])
        E("scalar", lambda e, a=rv: e.activation(a, a, AF.Sqrt),
          reads=[rowv[1]], writes=[rowv[1]])
        E("vector", lambda e, a=rr, v=rv: e.reciprocal(a, v),
          reads=[rowv[1]], writes=[rowr[1]])
        E("vector", lambda e, a=rm, r=rr: e.scalar_tensor_tensor(
            a, a, -1.0, r, ALU.mult, ALU.mult),
          reads=[rowr[1], rowm[1]], writes=[rowm[1]])
        for (o, n) in splits:
            E("tensor", lambda e, o=o, n=n, r=rr: e.matmul(
                PS[:, o:o + n], orow_t[:], r[0:1, o:o + n], start=True, stop=True),
              reads=[rowr[1], orB], writes=[psB[0] if o < 1024 else psB[1]])
            E("tensor", lambda e, o=o, n=n, r=rm: e.matmul(
                PS[:, 2048 + o:2048 + o + n], orow_t[:], r[0:1, o:o + n], start=True, stop=True),
              reads=[rowm[1], orB], writes=[psB[2] if o < 1024 else psB[3]])
        for c in range(CH):
            E("vector", lambda e, c=c: e.tensor_mul(yc(c), xc(c), PS[:, 0:COLS]),
              reads=[xB[c], psB[0], psB[1]], writes=[yB[c]])
            E("vector", lambda e, c=c: e.tensor_add(yc(c), yc(c), PS[:, 2048:2048 + COLS]),
              reads=[psB[2], psB[3], yB[c]], writes=[yB[c]])

    def spill_x():
        E("scalar", lambda e: e.dma_start(out=xsp_d, in_=rx_t[:]),
          reads=list(xB), writes=[xspB], dma=True, chan=11)

    def restore_x(seed):
        for c in range(CH):
            xB[c] = RX.buf(f"x{c}", seed)
        E("scalar", lambda e: e.dma_start(
            out=rxF[:, 0:CH * COLS], in_=xsp_d[:, 0:CH * COLS]),
          reads=[xspB], writes=list(xB), dma=True, chan=12)

    def carry_exchange(idx):
        cin, cout = cc_in[idx].ap(), cc_out[idx].ap()
        gma, gmb = RY.carve(4096 * 4, 2048, F32R, RY.fence(), "gm")
        gm = gma[0:8, :]
        E("scalar", lambda e, cin=cin: e.dma_start(out=cin, in_=tot_t[:]),
          reads=[totB], writes=[ccB[idx]], dma=True)
        E("gpsimd", lambda e, i=idx: e.collective_compute(
            "AllGather", ALU.bypass, [list(range(NCORE))],
            [cc_in[i].ap().opt()], [cc_out[i].ap().opt()]),
          reads=[ccB[idx]], writes=[ccB[idx]])
        E("scalar", lambda e, cout=cout, gm=gm: e.dma_start(
            out=gm, in_=cout.rearrange("(r p) c -> r (p c)", p=128).bitcast(F32R)),
          reads=[ccB[idx]], writes=[gmb], dma=True)
        E("vector", lambda e, gm=gm: e.tensor_scalar_mul(gm, gm, msk_t[:][:, 0:1]),
          reads=[mskB, gmb], writes=[gmb])
        gmr = gm.rearrange("r (p c) -> r c p", c=16)
        for m in range(CH):
            E("tensor", lambda e, m=m, g=gmr: e.matmul(
                PS[:, 3072 + 2 * m:3072 + 2 * m + 2], g[:, m, :], o82_t[:],
                start=True, stop=True),
              reads=[gmb, o82B], writes=[psB[3]])
        E("vector", lambda e: e.tensor_copy(car_t[:], PS[:, 3072:3104]),
          reads=[psB[3]], writes=[carB])

    def mm_acc(lhs_ap_fn, lhs_buf, rhs_fn, rhs_buf_fn, nk, split=1):
        psap, psb = ps_next()
        step = nk // split
        for s in range(split):
            fns = []
            ks = range(s * step, (s + 1) * step)
            for k in ks:
                for h in range(2):
                    fns.append(lambda e, k=k, h=h, p=psap: e.matmul(
                        p[:, h * 512:(h + 1) * 512], lhs_ap_fn(k), rhs_fn(k, h),
                        start=(k == 0), stop=(k == nk - 1)))
            pr.emit_group("tensor", fns,
                          reads=[lhs_buf] + [rhs_buf_fn(k) for k in ks],
                          writes=[psb])
        return psap, psb

    def y_rhs(k, h):
        return yc(k)[:, RL + h * 512: RL + (h + 1) * 512]

    def y_buf(k):
        return yB[k]

    ph = [0]

    def phase(f):
        if ph[0] < N_PHASES:
            f()
        ph[0] += 1

    phase(ln)

    def p_conv():
        seed = C1.fence()
        ct = [C1.carve(45056, TOK, F32, seed, "cv0"),
              C1.carve(49152, TOK, F32, seed, "cv1")]
        for c in range(CH):
            a, ab = ct[c % 2]
            cw0 = sm_t[:][:, S_CW + 3 * c: S_CW + 3 * c + 1]
            cw1 = sm_t[:][:, S_CW + 3 * c + 1: S_CW + 3 * c + 2]
            cw2 = sm_t[:][:, S_CW + 3 * c + 2: S_CW + 3 * c + 3]
            cbv = sm_t[:][:, S_CB + c: S_CB + c + 1]
            E("vector", lambda e, c=c, a=a, w=cw0, bv=cbv: e.tensor_scalar(
                a, yc(c)[:, 1:1025], w, bv, ALU.mult, ALU.add),
              reads=[yB[c], smB], writes=[ab])
            E("vector", lambda e, c=c, a=a, w=cw1: e.scalar_tensor_tensor(
                a, yc(c)[:, 2:1026], w, a, ALU.mult, ALU.add),
              reads=[yB[c], smB, ab], writes=[ab])
            E("vector", lambda e, c=c, a=a, w=cw2: e.scalar_tensor_tensor(
                a, yc(c)[:, 3:1027], w, a, ALU.mult, ALU.add),
              reads=[yB[c], smB, ab], writes=[ab])
            E("vector", lambda e, c=c, a=a: e.tensor_add(
                xc(c)[:, RL:RH], xc(c)[:, RL:RH], a),
              reads=[ab, xB[c]], writes=[xB[c]])
    phase(p_conv)

    def p_attn_a_proj():
        spill_x()
        seed = RX.fence()
        wsA = [RX.carve(0, 2048, F32R, seed, "wsA0"),
               RX.carve(8192, 2048, F32R, seed, "wsA1")]
        wsB = [RX.carve(16384, 2048, F32R, seed, "wsB0"),
               RX.carve(24576, 2048, F32R, seed, "wsB1")]
        wg2f = [RX.carve(32768, 512, F32R, seed, "wg2f0"),
                RX.carve(34816, 512, F32R, seed, "wg2f1")]
        gtmp = RX.carve(36864, TOK, F32R, seed, "gtmp")
        ugt = [RX.carve(40960, TOK, F32R, seed, "ugt0"),
               RX.carve(45056, TOK, F32R, seed, "ugt1")]
        cseed = C1.fence()
        vgc = [C1.carve(c * 2048, TOK, BF16, cseed, f"vg{c}") for c in range(CH)]
        hc = [C1.carve(32768 + k * 2048, TOK, BF16, cseed, f"h{k}") for k in range(HCH)]
        wg2b = [C1.carve(40960, 512, BF16, cseed, "wg2b0"),
                C1.carve(41984, 512, BF16, cseed, "wg2b1")]
        for m in range(HCH):
            buf = wsA[m % 2]
            E("sync", lambda e, m=m, a=buf[0]: e.dma_start(
                out=a, in_=d_in["wg1_p"][m].bitcast(F32R)), writes=[buf[1]], dma=True,
              chan=m % 2)
            psap, psb = mm_acc(
                lambda k, a=buf[0]: a[:, k * 128:(k + 1) * 128], buf[1],
                y_rhs, y_buf, CH)
            E("scalar", lambda e, m=m, p=psap, a=hc[m][0]: e.activation(
                a, p, AF.Relu, bias=sm_t[:][:, S_BG1 + m:S_BG1 + m + 1]),
              reads=[psb, smB], writes=[hc[m][1]])
        for m in range(CH):
            bufu, bufv = wsA[m % 2], wsB[m % 2]
            bufg, bufgb = wg2f[m % 2], wg2b[m % 2]
            E("sync", lambda e, m=m, a=bufg[0]: e.dma_start(
                out=a, in_=d_in["wg2_p"][m].bitcast(F32R)), writes=[bufg[1]], dma=True,
              chan=4 + m % 2)
            E("sync", lambda e, m=m, a=bufu[0]: e.dma_start(
                out=a, in_=d_in["wu_p"][m].bitcast(F32R)), writes=[bufu[1]], dma=True,
              chan=m % 2)
            E("sync", lambda e, m=m, a=bufv[0]: e.dma_start(
                out=a, in_=d_in["wv_p"][m].bitcast(F32R)), writes=[bufv[1]], dma=True,
              chan=2 + m % 2)
            E("scalar", lambda e, a=bufgb[0], f=bufg[0]: e.copy(a, f),
              reads=[bufg[1]], writes=[bufgb[1]])
            psg, psgb = mm_acc(
                lambda k, a=bufgb[0]: a[:, k * 128:(k + 1) * 128], bufgb[1],
                lambda k, h: hc[k][0][:, h * 512:(h + 1) * 512],
                lambda k: hc[k][1], HCH)
            E("scalar", lambda e, a=gtmp[0], p=psg, m=m: e.activation(
                a, p, AF.Sigmoid, bias=sm_t[:][:, S_BG2 + m:S_BG2 + m + 1]),
              reads=[psgb, smB], writes=[gtmp[1]])
            psu, psub = mm_acc(
                lambda k, a=bufu[0]: a[:, k * 128:(k + 1) * 128], bufu[1],
                y_rhs, y_buf, CH)
            ug = ugt[m % 2]
            E("vector", lambda e, a=ug[0], p=psu, m=m, g=gtmp[0]: e.scalar_tensor_tensor(
                a, p, sm_t[:][:, S_BU + m:S_BU + m + 1], g, ALU.add, ALU.mult),
              reads=[psub, smB, gtmp[1]], writes=[ug[1]])
            E("vector", lambda e, a=ug[0], m=m: e.reduce_sum(
                tot_t[:][:, m:m + 1], a, axis=AX),
              reads=[ug[1]], writes=[totB])
            E("scalar", lambda e, a=ug[0], m=m: e.dma_start(
                out=ugd_d[m * 128:(m + 1) * 128, :], in_=a),
              reads=[ug[1]], writes=[ugdB], dma=True, chan=6 + m % 2)
            psv, psvb = mm_acc(
                lambda k, a=bufv[0]: a[:, k * 128:(k + 1) * 128], bufv[1],
                y_rhs, y_buf, CH)
            E("vector", lambda e, m=m, p=psv, g=gtmp[0], a=vgc[m][0]: e.scalar_tensor_tensor(
                a, p, sm_t[:][:, S_BV + m:S_BV + m + 1], g, ALU.add, ALU.mult),
              reads=[psvb, smB, gtmp[1]], writes=[vgc[m][1]])
        return vgc
    vgc_holder = []
    phase(lambda: vgc_holder.append(p_attn_a_proj()))

    gc_holder = []

    def p_attn_a_scan():
        vgc = vgc_holder[0]
        carry_exchange(0)
        cseed = C1.fence()
        ugr = [C1.carve(45056, TOK, F32R, cseed, "ugr0"),
               C1.carve(49152, TOK, F32R, cseed, "ugr1")]
        ktmp = C1.carve(53248, TOK, F32R, cseed, "ktmp")
        yseed = RY.fence()
        gc = [RY.carve(c * 4096, TOK, F32R, yseed, f"g{c}") for c in range(CH)]
        gc_holder.append(gc)
        for m in range(CH):
            u = ugr[m % 2]
            E("scalar", lambda e, a=u[0], m=m: e.dma_start(
                out=a, in_=ugd_d[m * 128:(m + 1) * 128, :]),
              reads=[ugdB], writes=[u[1]], dma=True, chan=6 + m % 2)
            E("vector", lambda e, a=ktmp[0], u=u[0], m=m: e.tensor_tensor_scan(
                a, u, u, car_t[:][:, 2 * m:2 * m + 1], ALU.add, ALU.bypass),
              reads=[u[1], carB], writes=[ktmp[1]])
            E("vector", lambda e, m=m, a=ktmp[0], g=gc[m][0], v=vgc[m][0]: e.tensor_mul(
                g, a, v),
              reads=[ktmp[1], vgc[m][1]], writes=[gc[m][1]])
    phase(p_attn_a_scan)

    def p_attn_a_uv():
        kuv = os.environ.get("KUV", "all")
        gc = gc_holder[0]
        seed = C2.fence()
        ws = [C2.carve(0, 2048, F32R, seed, "uvw0"),
              C2.carve(8192, 2048, F32R, seed, "uvw1")]
        if kuv != "nores":
            restore_x(RX.fence())
        for mo in range(CH):
            if kuv == "restonly":
                continue
            buf = ws[mo % 2]
            E("sync", lambda e, mo=mo, a=buf[0]: e.dma_start(
                out=a, in_=d_in["wu_p"][mo].bitcast(F32R)), writes=[buf[1]], dma=True,
              chan=mo % 2)
            psap, psb = mm_acc(
                lambda k, a=buf[0]: a[:, k * 128:(k + 1) * 128], buf[1],
                lambda k, h: gc[k][0][:, h * 512:(h + 1) * 512],
                lambda k: gc[k][1], CH, split=2)
            if kuv == "all":
                E("vector", lambda e, mo=mo, p=psap: e.tensor_add(
                    xc(mo)[:, RL:RH], xc(mo)[:, RL:RH], p),
                  reads=[psb, xB[mo]], writes=[xB[mo]])
    phase(p_attn_a_uv)

    def p_ffn(final):
        kffn = os.environ.get("KFFN", "all") if final else "all"
        if kffn in ("xpre", "xln", "xspill", "xdump"):
            if kffn == "xln":
                ln()
            if kffn == "xspill":
                spill_x()
                restore_x(RX.fence())
            if kffn == "xdump":
                spill_x()
                for mo in range(CH):
                    E("sync", lambda e, mo=mo: e.dma_start(
                        out=out_d[mo * 128:(mo + 1) * 128, :].bitcast(F32R),
                        in_=xsp_d[:, mo * COLS + RL: mo * COLS + RH]),
                      reads=[xspB], writes=[outB], dma=True)
                E("vector", lambda e: e.memset(end_t[0:1, 0:1], 0.0), reads=[outB])
                return
            yseed = RY.fence()
            otl = [RY.carve(0, TOK, F32R, yseed, "ot0"),
                   RY.carve(4096, TOK, F32R, yseed, "ot1")]
            for mo in range(CH):
                o = otl[mo % 2]
                E("vector", lambda e, mo=mo, a=o[0]: e.tensor_scalar_mul(
                    a, xc(mo)[:, RL:RH], 1.0),
                  reads=[xB[mo]], writes=[o[1]])
                E("sync", lambda e, mo=mo, a=o[0]: e.dma_start(
                    out=out_d[mo * 128:(mo + 1) * 128, :].bitcast(F32R), in_=a),
                  reads=[o[1]], writes=[outB], dma=True)
            return
        ln()
        spill_x()
        xseed = RX.fence()
        g0 = [RX.carve(0, 2048, F32R, xseed, "g0a"),
              RX.carve(8192, 2048, F32R, xseed, "g0b")]
        g1 = [RX.carve(16384, 2048, F32R, xseed, "g1a"),
              RX.carve(24576, 2048, F32R, xseed, "g1b")]
        gel = RX.carve(32768, TOK, F32R, xseed, "gel")
        cseed = C1.fence()
        a8 = [C1.carve(j * 1024, TOK, FP8, cseed, f"a8{j}") for j in range(JCH)]
        for j in range(JCH if kffn != "xonly" else 0):
            b0, b1 = g0[j % 2], g1[j % 2]
            E("sync", lambda e, j=j, a=b0[0]: e.dma_start(
                out=a, in_=d_in["g0_p"][j].bitcast(F32R)), writes=[b0[1]], dma=True,
              chan=j % 2)
            E("sync", lambda e, j=j, a=b1[0]: e.dma_start(
                out=a, in_=d_in["g1_p"][j].bitcast(F32R)), writes=[b1[1]], dma=True,
              chan=2 + j % 2)
            psg, psgb = mm_acc(
                lambda k, a=b0[0]: a[:, k * 128:(k + 1) * 128], b0[1],
                y_rhs, y_buf, CH)
            E("scalar", lambda e, a=gel[0], p=psg: e.activation(a, p, AF.Gelu_apprx_tanh),
              reads=[psgb], writes=[gel[1]])
            psf, psfb = mm_acc(
                lambda k, a=b1[0]: a[:, k * 128:(k + 1) * 128], b1[1],
                y_rhs, y_buf, CH)
            E("vector", lambda e, j=j, a=gel[0], p=psf, o=a8[j][0]: e.tensor_mul(o, a, p),
              reads=[gel[1], psfb], writes=[a8[j][1]])
        c2seed = C2.fence()
        vfq = C2.carve(0, 2048, F32R, c2seed, "vfq")
        v8 = [C1.carve(65536, 2048, FP8, C1.fence(), "v80"),
              C1.carve(67584, 2048, FP8, (), "v81")]
        otl = None
        if final:
            yseed = RY.fence()
            otl = [RY.carve(0, TOK, F32R, yseed, "ot0"),
                   RY.carve(4096, TOK, F32R, yseed, "ot1")]
        restore_x(RX.fence())
        if kffn == "xonly":
            for mo in range(CH):
                o = otl[mo % 2]
                E("vector", lambda e, mo=mo, a=o[0]: e.tensor_scalar_mul(
                    a, xc(mo)[:, RL:RH], 1.0),
                  reads=[xB[mo]], writes=[o[1]])
                E("sync", lambda e, mo=mo, a=o[0]: e.dma_start(
                    out=out_d[mo * 128:(mo + 1) * 128, :].bitcast(F32R), in_=a),
                  reads=[o[1]], writes=[outB], dma=True)
            return
        for mo in range(CH):
            psap, psb = ps_next()
            for qq in range(4):
                q8 = v8[qq % 2]
                E("sync", lambda e, mo=mo, qq=qq, a=vfq[0]: e.dma_start(
                    out=a, in_=d_in["lin_p"][mo][:, qq * 2048:(qq + 1) * 2048].bitcast(F32R)),
                  reads=[], writes=[vfq[1]], dma=True, chan=4)
                E("scalar", lambda e, a=q8[0], f=vfq[0]: e.copy(a, f),
                  reads=[vfq[1]], writes=[q8[1]])
                fns = []
                abufs = []
                for jj2 in range(8):
                    j0 = qq * 16 + jj2 * 2
                    abufs.append(a8[j0][1])
                    abufs.append(a8[j0 + 1][1])
                    lhs = q8[0][:, jj2 * 256:(jj2 + 1) * 256].rearrange(
                        "p (j m) -> p j m", j=2)
                    rhs_pair = c1_t[:][:, j0 * 1024:(j0 + 2) * 1024].rearrange(
                        "p (j t) -> p j t", j=2)
                    for h in range(2):
                        fns.append(lambda e, j0=j0, h=h, l=lhs, r=rhs_pair, p=psap: e.matmul(
                            p[:, h * 512:(h + 1) * 512], l, r[:, :, h * 512:(h + 1) * 512],
                            start=(j0 == 0), stop=(j0 == JCH - 2),
                            perf_mode=mybir.MatmulPerfMode.DoubleRow))
                pr.emit_group("tensor", fns, reads=[q8[1]] + abufs, writes=[psb])
            if final:
                o = otl[mo % 2]
                E("vector", lambda e, mo=mo, p=psap, a=o[0]: e.tensor_add(
                    a, xc(mo)[:, RL:RH], p),
                  reads=[psb, xB[mo]], writes=[o[1]])
                E("sync", lambda e, mo=mo, a=o[0]: e.dma_start(
                    out=out_d[mo * 128:(mo + 1) * 128, :].bitcast(F32R), in_=a),
                  reads=[o[1]], writes=[outB], dma=True)
            else:
                E("vector", lambda e, mo=mo, p=psap: e.tensor_add(
                    xc(mo)[:, RL:RH], xc(mo)[:, RL:RH], p),
                  reads=[psb, xB[mo]], writes=[xB[mo]])
    phase(lambda: p_ffn(False))

    def p_attn_b():
        katb = os.environ.get("KATB", "all")
        ln()
        spill_x()
        xseed = RX.fence()
        wq = [RX.carve(0, 2048, F32R, xseed, "wq0"),
              RX.carve(8192, 2048, F32R, xseed, "wq1")]
        wk = [RX.carve(16384, 2048, F32R, xseed, "wk0"),
              RX.carve(24576, 2048, F32R, xseed, "wk1")]
        wv = [RX.carve(32768, 2048, F32R, xseed, "wv0"),
              RX.carve(40960, 2048, F32R, xseed, "wv1")]
        ktm = RX.carve(49152, TOK, F32R, xseed, "ktm")
        qtm = [RX.carve(53248, TOK, F32R, xseed, "qt0"),
               RX.carve(57344, TOK, F32R, xseed, "qt1")]
        cseed = C1.fence()
        kvc = [C1.carve(c * 4096, TOK, F32, cseed, f"kv{c}") for c in range(CH)]
        if katb == "ln":
            return
        nM = int(katb[1:]) if katb.startswith("m") else CH
        for m in range(nM):
            bq, bk, bv = wq[m % 2], wk[m % 2], wv[m % 2]
            E("sync", lambda e, m=m, a=bq[0]: e.dma_start(
                out=a, in_=d_in["wq_p"][m].bitcast(F32R)), writes=[bq[1]], dma=True,
              chan=m % 2)
            E("sync", lambda e, m=m, a=bk[0]: e.dma_start(
                out=a, in_=d_in["wk_p"][m].bitcast(F32R)), writes=[bk[1]], dma=True,
              chan=2 + m % 2)
            E("sync", lambda e, m=m, a=bv[0]: e.dma_start(
                out=a, in_=d_in["wvb_p"][m].bitcast(F32R)), writes=[bv[1]], dma=True,
              chan=4 + m % 2)
            if katb != "noq":
                psq, psqb = mm_acc(
                    lambda k, a=bq[0]: a[:, k * 128:(k + 1) * 128], bq[1],
                    y_rhs, y_buf, CH)
                if katb != "qne":
                    qb = qtm[m % 2]
                    E("scalar", lambda e, a=qb[0], p=psq: e.activation(a, p, AF.Copy, scale=DENOM),
                      reads=[psqb], writes=[qb[1]])
                    if katb != "qns":
                        E("sync", lambda e, a=qb[0], m=m: e.dma_start(
                            out=qd_d[m * 128:(m + 1) * 128, :], in_=a),
                          reads=[qb[1]], writes=[qdB], dma=True)
            psk, pskb = mm_acc(
                lambda k, a=bk[0]: a[:, k * 128:(k + 1) * 128], bk[1],
                y_rhs, y_buf, CH)
            E("vector", lambda e, a=ktm[0], p=psk: e.tensor_copy(a, p),
              reads=[pskb], writes=[ktm[1]])
            psv, psvb = mm_acc(
                lambda k, a=bv[0]: a[:, k * 128:(k + 1) * 128], bv[1],
                y_rhs, y_buf, CH)
            E("vector", lambda e, m=m, a=ktm[0], p=psv, o=kvc[m][0]: e.tensor_mul(o, a, p),
              reads=[ktm[1], psvb], writes=[kvc[m][1]])
            E("vector", lambda e, m=m, o=kvc[m][0]: e.reduce_sum(
                tot_t[:][:, m:m + 1], o, axis=AX),
              reads=[kvc[m][1]], writes=[totB])
        if katb == "proj":
            return
        carry_exchange(1)
        if katb == "cc":
            return
        yseed = RY.fence()
        vh = [RY.carve(c * 4096, TOK, F32R, yseed, f"vh{c}") for c in range(CH)]
        for m in range(CH):
            qb = qtm[m % 2]
            E("scalar", lambda e, a=qb[0], m=m: e.dma_start(
                out=a, in_=qd_d[m * 128:(m + 1) * 128, :]),
              reads=[qdB], writes=[qb[1]], dma=True, chan=8 + m % 2)
            E("vector", lambda e, a=ktm[0], m=m, u=kvc[m][0]: e.tensor_tensor_scan(
                a, u, u, car_t[:][:, 2 * m:2 * m + 1], ALU.add, ALU.bypass),
              reads=[kvc[m][1], carB], writes=[ktm[1]])
            E("vector", lambda e, m=m, q=qb[0], a=ktm[0], o=vh[m][0]: e.tensor_mul(o, q, a),
              reads=[qb[1], ktm[1]], writes=[vh[m][1]])
        if katb == "scan":
            return
        c2seed = C2.fence()
        wo = [C2.carve(0, 2048, F32R, c2seed, "wo0"),
              C2.carve(8192, 2048, F32R, c2seed, "wo1")]
        restore_x(RX.fence())
        for mo in range(CH):
            buf = wo[mo % 2]
            E("sync", lambda e, mo=mo, a=buf[0]: e.dma_start(
                out=a, in_=d_in["wo_p"][mo].bitcast(F32R)), writes=[buf[1]], dma=True)
            psap, psb = mm_acc(
                lambda k, a=buf[0]: a[:, k * 128:(k + 1) * 128], buf[1],
                lambda k, h: vh[k][0][:, h * 512:(h + 1) * 512],
                lambda k: vh[k][1], CH)
            E("vector", lambda e, mo=mo, p=psap: e.tensor_add(
                xc(mo)[:, RL:RH], xc(mo)[:, RL:RH], p),
              reads=[psb, xB[mo]], writes=[xB[mo]])
    phase(p_attn_b)

    phase(lambda: p_ffn(True))

    if N_PHASES < 8:
        yseed = RY.fence()
        ot = [RY.carve(0, TOK, F32R, yseed, "dbg0"),
              RY.carve(4096, TOK, F32R, yseed, "dbg1")]
        for mo in range(CH):
            o = ot[mo % 2]
            E("vector", lambda e, mo=mo, a=o[0]: e.tensor_scalar_mul(
                a, xc(mo)[:, RL:RH], 1.0),
              reads=[xB[mo]], writes=[o[1]])
            E("sync", lambda e, mo=mo, a=o[0]: e.dma_start(
                out=out_d[mo * 128:(mo + 1) * 128, :].bitcast(F32R), in_=a),
              reads=[o[1]], writes=[outB], dma=True)

    fin = [(e, pr.counts[e]) for e in SEMS if e != "vector" and pr.counts[e] > 0]
    endB = Buf("end")
    E("vector", lambda e: e.memset(end_t[:], 0.0), writes=[endB])
    pr.ops["vector"].append((lambda e: e.memset(end_t[:], 1.0), fin, 1, "vector"))
    pr.counts["vector"] += 1

    with nc.allow_low_precision(reason="deliberate f32r/bf16/fp8 stores; f32 PSUM accumulation throughout"):
        pr.build(nc, sems)
    ctx.close()
    return nc


def _pack2d(A):
    K, M = A.shape
    return np.ascontiguousarray(
        A.reshape(K // 128, 128, M // 128, 128).transpose(2, 1, 0, 3)
        .reshape(M // 128, 128, K))


_NC_CACHE = [None]


def kernel(L0_attn_a_x, L0_attn_a_Wu, L0_attn_a_bu, L0_attn_a_Wv, L0_attn_a_bv,
           L0_attn_a_Wg1, L0_attn_a_bg1, L0_attn_a_Wg2, L0_attn_a_bg2,
           L0_attn_a_conv_w, L0_attn_a_conv_b, L0_attn_b_Wq, L0_attn_b_Wk,
           L0_attn_b_Wv, L0_attn_b_Wo, L0_ffn_w_gating, L0_ffn_w_linear):
    f = np.float32
    x = np.asarray(L0_attn_a_x, f)
    Wu = np.asarray(L0_attn_a_Wu, f); Wv = np.asarray(L0_attn_a_Wv, f)
    Wg1 = np.asarray(L0_attn_a_Wg1, f); Wg2 = np.asarray(L0_attn_a_Wg2, f)
    Wq = np.asarray(L0_attn_b_Wq, f); Wk = np.asarray(L0_attn_b_Wk, f)
    Wvb = np.asarray(L0_attn_b_Wv, f); Wo = np.asarray(L0_attn_b_Wo, f)
    wg = np.asarray(L0_ffn_w_gating, f); wl = np.asarray(L0_ffn_w_linear, f)

    packs = {
        "wu_p": _pack2d(np.ascontiguousarray(Wu.T)),
        "wv_p": _pack2d(np.ascontiguousarray(Wv.T)),
        "wg1_p": _pack2d(np.ascontiguousarray(Wg1.T)),
        "wg2_p": _pack2d(np.ascontiguousarray(Wg2.T)),
        "wq_p": _pack2d(np.ascontiguousarray(Wq.T)),
        "wk_p": _pack2d(np.ascontiguousarray(Wk.T)),
        "wvb_p": _pack2d(np.ascontiguousarray(Wvb.T)),
        "wo_p": _pack2d(np.ascontiguousarray(Wo.T)),
        "g0_p": _pack2d(wg[0]),
        "g1_p": _pack2d(wg[1]),
        "lin_p": _pack2d(wl),
    }
    smalls = np.zeros((128, 120), f)
    smalls[:, S_BU:S_BU + 16] = np.asarray(L0_attn_a_bu, f).reshape(16, 128).T
    smalls[:, S_BV:S_BV + 16] = np.asarray(L0_attn_a_bv, f).reshape(16, 128).T
    smalls[:, S_CB:S_CB + 16] = np.asarray(L0_attn_a_conv_b, f).reshape(16, 128).T
    smalls[:, S_BG2:S_BG2 + 16] = np.asarray(L0_attn_a_bg2, f).reshape(16, 128).T
    smalls[:, S_BG1:S_BG1 + 4] = np.asarray(L0_attn_a_bg1, f).reshape(4, 128).T
    cw = np.asarray(L0_attn_a_conv_w, f).reshape(2048, 3)
    smalls[:, S_CW:S_CW + 48] = cw.reshape(16, 128, 3).transpose(1, 0, 2).reshape(128, 48)

    onescol = np.ones((128, 2), f)
    onesrow = np.ones((1, 128), f)
    ones82 = np.ones((8, 2), f)

    xpad = np.zeros((B, L + 2, D), f)
    xpad[:, 1:L + 1] = x

    in_maps = []
    for c in range(NCORE):
        s, q = c // 4, c % 4
        xTc = np.zeros((COLS, D), f)
        xTc[1:TOK + 3] = xpad[s, q * TOK: q * TOK + TOK + 2, :]
        xT = np.ascontiguousarray(xTc.T)
        mask = np.zeros((8, 1), f)
        for r in range(NCORE):
            if r // 4 == s and r < c:
                mask[r, 0] = 1.0
        m = dict(packs)
        m.update({"xT": xT, "smalls": smalls, "onescol": onescol,
                  "onesrow": onesrow, "ones82": ones82, "mask8": mask})
        in_maps.append(m)

    if _NC_CACHE[0] is None:
        _NC_CACHE[0] = build_nc()
    nc = _NC_CACHE[0]
    res = run_bass_kernel_spmd(nc, in_maps, core_ids=list(range(NCORE)))

    y = np.empty((B, L, D), f)
    for c in range(NCORE):
        s, q = c // 4, c % 4
        y[s, q * TOK:(q + 1) * TOK, :] = res.results[c]["out"].T
    return y
